# version 27
# speedup vs baseline: 1.2114x; 1.0076x over previous
import contextlib
import math

import numpy as np

import concourse.bass as bass
import concourse.mybir as mybir
from concourse.bass_utils import run_bass_kernel_spmd

F32 = mybir.dt.float32
BF16 = mybir.dt.bfloat16
AF = mybir.ActivationFunctionType
ALU = mybir.AluOpType
AX = mybir.AxisListType

T = 4096
D = 1024
H = 4
DH = 256
CH = 128
NCH = T // CH
S1 = 512
NSEG1 = T // S1
S3 = 256
NSEG3 = T // S3
S4 = 256
NSEG4 = T // S4
FF = 4096

ENGS = ("pe", "act", "dve", "pool", "sp")
DMAQ = {"dsp": "sp", "dact": "act", "dpool": "pool"}
ALLC = list(ENGS) + list(DMAQ)


def _nm(t):
    if hasattr(t, "tensor"):
        t = t.tensor
    return t.name


class Prog:

    R = 8

    def __init__(self, nc):
        self.nc = nc
        self.ops = {e: [] for e in ENGS}
        self.cnt = {e: 0 for e in ALLC}
        self.vc = {e: {} for e in ENGS}
        self.hist = {e: [] for e in ALLC}
        self.writer = {}
        self.readers = {}
        self.sems = {}
        self.n_ops = 0

    def _q_waits(self, q, k):
        R = self.R
        out = []
        for i in range(min(k, R)):
            cnt_i = (k - 1 - i) // R + 1
            out.append((f"{q}{i}", 16 * cnt_i))
        return out

    def _need_for(self, reads, writes):
        need = {}

        def add(f, c):
            need[f] = max(need.get(f, 0), c)

        for t in reads:
            w = self.writer.get(_nm(t))
            if w:
                add(*w)
        for t in writes:
            w = self.writer.get(_nm(t))
            if w:
                add(*w)
            for f, c in self.readers.get(_nm(t), {}).items():
                add(f, c)
        return need

    def _waits(self, e, need):
        waits = []
        myvc = self.vc[e]
        for f in sorted(need):
            c = need[f]
            if c > self.cnt[f] + (1 if f == e else 0):
                raise RuntimeError(f"op on {e} needs {f}@{c} but {f} only at {self.cnt[f]} (missing milestone inc)")
            if f == e and (c > self.cnt[f] or e == "pe"):
                continue
            if myvc.get(f, 0) < c:
                if f in DMAQ:
                    k = c // 16
                    for (sname, sval) in self._q_waits(f, k):
                        if myvc.get(sname, 0) < sval:
                            waits.append((sname, sval))
                            myvc[sname] = sval
                    for idx in range(myvc.get(f, 0) // 16, k):
                        h = self.hist[f][idx]
                        for kk, v in h.items():
                            if myvc.get(kk, 0) < v:
                                myvc[kk] = v
                else:
                    waits.append((f, c))
                    h = self.hist[f][c - 1]
                    for kk, v in h.items():
                        if myvc.get(kk, 0) < v:
                            myvc[kk] = v
                if myvc.get(f, 0) < c:
                    myvc[f] = c
        return waits

    def _record(self, e, c_eff, reads, writes):
        for t in writes:
            self.writer[_nm(t)] = (e, c_eff)
            self.readers.pop(_nm(t), None)
        for t in reads:
            self.readers.setdefault(_nm(t), {})[e] = c_eff

    @staticmethod
    def _is_psum(t):
        h = t.tensor if hasattr(t, "tensor") else t
        return "PSum" in type(h).__name__ or "psum" in str(getattr(h, "space", "")).lower()

    def op(self, e, fn, reads=(), writes=(), inc=True):
        reads = list(reads)
        writes = list(writes) + [t for t in reads if self._is_psum(t)]
        reads = [t for t in reads if not self._is_psum(t)]
        need = self._need_for(reads, writes)
        waits = self._waits(e, need)
        incs = []
        if inc:
            self.cnt[e] += 1
            h = dict(self.vc[e])
            h[e] = self.cnt[e]
            self.hist[e].append(h)
            incs = [(e, 1)]
            c_eff = self.cnt[e]
        else:
            c_eff = self.cnt[e] + 1
        self.ops[e].append((waits, fn, incs))
        self._record(e, c_eff, reads, writes)
        self.n_ops += 1

    def dma(self, q, out_ap, in_ap, reads=(), writes=(), **kw):
        e = DMAQ[q]
        reads = list(reads) + [in_ap]
        writes = list(writes) + [out_ap]
        need = self._need_for(reads, writes)
        waits = self._waits(e, need)
        k = self.cnt[q] // 16 + 1
        sub = (k - 1) % self.R
        sname = f"{q}{sub}"
        if k > self.R:
            prev_target = 16 * ((k - 1) // self.R)
            myvc = self.vc[e]
            if myvc.get(sname, 0) < prev_target:
                waits.append((sname, prev_target))
                myvc[sname] = prev_target
                h = self.hist[q][k - self.R - 1]
                for kk, v in h.items():
                    if myvc.get(kk, 0) < v:
                        myvc[kk] = v
        self.cnt[q] += 16
        qvc = dict(self.vc[e])
        qvc[sname] = 16 * ((k - 1) // self.R + 1)
        self.hist[q].append(qvc)
        c = self.cnt[q]
        self.ops[e].append((waits, lambda eng, o=out_ap, i=in_ap, kw2=dict(kw): eng.dma_start(out=o, in_=i, **kw2), [(sname, 16)]))
        self._record(q, c, reads, writes)
        self.n_ops += 1

    def barrier(self):
        snapshot = {f: self.cnt[f] for f in ALLC}
        for e in ENGS:
            waits = []
            for f, c in snapshot.items():
                if f == e or c == 0:
                    continue
                if f in DMAQ:
                    waits.extend(self._q_waits(f, c // 16))
                else:
                    waits.append((f, c))
                self.vc[e][f] = max(self.vc[e].get(f, 0), c)
            for f, c in snapshot.items():
                if f in DMAQ:
                    for idx in range(c // 16):
                        for kk, v in self.hist[f][idx].items():
                            if self.vc[e].get(kk, 0) < v:
                                self.vc[e][kk] = v
                elif c > 0 and f != e:
                    for kk, v in self.hist[f][c - 1].items():
                        if self.vc[e].get(kk, 0) < v:
                            self.vc[e][kk] = v
            self.cnt[e] += 1
            h = dict(self.vc[e])
            h[e] = self.cnt[e]
            self.hist[e].append(h)
            self.ops[e].append((waits, self._flush_fn(e), [(e, 1)]))

    def _flush_fn(self, e):
        if e in ("pe", "sp"):
            return lambda eng: eng.nop(nofuse=True)
        dummy = self.dummy[e]
        if e == "act":
            return lambda eng: eng.memzero(dummy[:])
        return lambda eng: eng.memset(dummy[:], 0.0)

    def barrier_dma_done(self):
        need = {q: self.cnt[q] for q in DMAQ if self.cnt[q] > 0}
        waits = self._waits("sp", need)
        self.ops["sp"].append((waits, lambda eng: eng.nop(nofuse=True), []))

    def emit(self):
        nc = self.nc
        with nc.Block() as block:
            def mk(ename):
                def body(eng):
                    for waits, fn, incs in self.ops[ename]:
                        for (f, v) in waits:
                            eng.wait_ge(self.sems[f], v)
                        inst = fn(eng)
                        for (f, a) in incs:
                            inst.then_inc(self.sems[f], a)
                return body
            block.tensor(mk("pe"))
            block.scalar(mk("act"))
            block.vector(mk("dve"))
            block.gpsimd(mk("pool"))
            block.sync(mk("sp"))



def build_program(debug_taps=(), max_phase=9, p2_chunks=NCH, p2_stage=9, p3a_segs=NSEG3, p3b_segs=NSEG3, p3b_weights=True, p3a_stage=9, skip_fold=False):
    nc = bass.Bass("TRN2", target_bir_lowering=False)

    def din(name, shape, dt=F32):
        return nc.dram_tensor(name, shape, dt, kind="ExternalInput").ap()

    x_h = din("x", [T, D])
    wq_h = din("wq_t", [D, D])
    wk_h = din("wk_t", [D, D])
    wv_h = din("wv_t", [D, D])
    wb_h = din("wb_t", [D, H])
    qc_h = din("qconv", [D, 4])
    kc_h = din("kconv", [D, 4])
    vc_h = din("vconv", [D, 4])
    firS_h = din("firS", [D, 5])
    firL_h = din("firL", [D, 64])
    w1x_h = din("w1x_t", [D, D])
    w1s_h = din("w1s_t", [16, D])
    fb1_h = din("fuse_b1v", [128, 8])
    w2g_h = din("w2g_t", [D, H])
    fb2_h = din("fuse_b2v", [1, H])
    ltemp_h = din("ltemp", [1, 1])
    rs_h = din("rs", [1, 1])
    rl_h = din("rl", [1, 1])
    onorm_h = din("onorm", [128, 2])
    wo_h = din("wo_t", [D, D])
    fw1_h = din("fw1", [D, FF])
    fb1f_h = din("fb1", [128, 32])
    fw2_h = din("fw2", [FF, D])
    fb2f_h = din("fb2", [1, D])
    gamma_h = din("gamma", [1, D])
    beta_h = din("beta", [1, D])

    out_h = nc.dram_tensor("out", [T, D], F32, kind="ExternalOutput").ap()

    def scratch(name, shape, dt):
        kind = "ExternalOutput" if name in debug_taps else "Internal"
        return nc.dram_tensor(name, shape, dt, kind=kind).ap()

    s_qf = scratch("s_qf", [8, 128, T], BF16)
    s_kf = scratch("s_kf", [8, 128, T], BF16)
    s_vf = scratch("s_vf", [8, 128, 128 + T], BF16)
    s_vtok = scratch("s_vtok", [T, D], F32)
    s_blog = scratch("s_blog", [H, T], F32)
    s_delta = scratch("s_delta", [T, D], F32)
    s_x2 = scratch("s_x2", [T, D], F32)
    s_ls = scratch("s_ls", [T, D], F32)
    s_ll = scratch("s_ll", [T, D], F32)
    s_dbg1 = scratch("s_dbg1", [T, 64], F32)
    s_dbg2 = scratch("s_dbg2", [T, 16], F32)
    s_dbg3 = scratch("s_dbg3", [T, D], F32)
    s_dbg4 = scratch("s_dbg4", [T, D], F32)
    s_dbg5 = scratch("s_dbg5", [D, D], F32)
    s_dbg7 = scratch("s_dbg7", [D, D], F32)
    s_dbg6 = scratch("s_dbg6", [T, D], F32)

    p = Prog(nc)
    st = contextlib.ExitStack()
    with st:
        for e in ENGS:
            p.sems[e] = st.enter_context(nc.semaphore(f"sem_{e}"))
        p.dummy = {e: st.enter_context(nc.sbuf_tensor(f"dummy_{e}", [1, 8], F32))
                   for e in ("act", "dve", "pool")}
        for q in DMAQ:
            for i in range(Prog.R):
                p.sems[f"{q}{i}"] = st.enter_context(nc.semaphore(f"sem_{q}{i}"))

        sb = lambda name, shape, dt: st.enter_context(nc.sbuf_tensor(name, shape, dt))
        ps = lambda name, shape, dt: st.enter_context(nc.psum_tensor(name, shape, dt))

        identb = sb("identb", [128, 128], BF16)
        identf = sb("identf", [128, 128], F32)
        did2 = sb("did2", [128, 64], BF16)
        mask_lt = sb("mask_lt", [128, 128], F32)
        mask_ut = sb("mask_ut", [128, 128], F32)
        ones1f = sb("ones1f", [1, 128], F32)
        tiny = sb("tiny", [1, 8], F32)
        itemp_b = sb("itemp_b", [128, 1], F32)
        rs_b = sb("rs_b", [128, 1], F32)
        rl_b = sb("rl_b", [128, 1], F32)
        b2g_b = sb("b2g_b", [128, H], F32)
        onorm_s = sb("onorm_s", [128, 2], F32)

        p.op("pool", lambda e: e.memset(identb[:], 1.0), writes=[identb])
        p.op("pool", lambda e: e.affine_select(out=identb[:], in_=identb[:], compare_op=ALU.is_equal,
                                               fill=0.0, base=0, pattern=[[-1, 128]], channel_multiplier=1),
             reads=[identb], writes=[identb])
        p.op("pool", lambda e: e.memset(identf[:], 1.0), writes=[identf])
        p.op("pool", lambda e: e.affine_select(out=identf[:], in_=identf[:], compare_op=ALU.is_equal,
                                               fill=0.0, base=0, pattern=[[-1, 128]], channel_multiplier=1),
             reads=[identf], writes=[identf])
        p.op("pool", lambda e: e.memset(mask_lt[:], 1.0), writes=[mask_lt])
        p.op("pool", lambda e: e.affine_select(out=mask_lt[:], in_=mask_lt[:], compare_op=ALU.is_gt,
                                               fill=0.0, base=0, pattern=[[-1, 128]], channel_multiplier=1),
             reads=[mask_lt], writes=[mask_lt])
        p.op("pool", lambda e: e.memset(mask_ut[:], 1.0), writes=[mask_ut])
        p.op("pool", lambda e: e.affine_select(out=mask_ut[:], in_=mask_ut[:], compare_op=ALU.is_gt,
                                               fill=0.0, base=1, pattern=[[1, 128]], channel_multiplier=-1),
             reads=[mask_ut], writes=[mask_ut])
        p.op("pool", lambda e: e.memset(ones1f[:], 1.0), writes=[ones1f])
        p.dma("dpool", did2[0:64, :], identb[0:64, 0:64])
        p.dma("dpool", did2[64:128, :], identb[0:64, 0:64])

        p.dma("dsp", tiny[:, 0:1], ltemp_h)
        p.op("act", lambda e: e.activation(tiny[:, 1:2], tiny[:, 0:1], AF.Abs), reads=[tiny], writes=[tiny])
        p.op("act", lambda e: e.activation(tiny[:, 2:3], tiny[:, 1:2], AF.Exp, scale=-1.0), reads=[tiny], writes=[tiny])
        p.op("act", lambda e: e.activation(tiny[:, 3:4], tiny[:, 2:3], AF.Ln, bias=1.0), reads=[tiny], writes=[tiny])
        p.op("act", lambda e: e.activation(tiny[:, 4:5], tiny[:, 0:1], AF.Relu), reads=[tiny], writes=[tiny])
        p.op("dve", lambda e: e.tensor_add(tiny[:, 5:6], tiny[:, 3:4], tiny[:, 4:5]), reads=[tiny], writes=[tiny])
        p.op("dve", lambda e: e.tensor_scalar(tiny[:, 6:7], tiny[:, 5:6], 1e-4, None, ALU.add, ALU.bypass),
             reads=[tiny], writes=[tiny])
        p.op("dve", lambda e: e.reciprocal(tiny[:, 7:8], tiny[:, 6:7]), reads=[tiny], writes=[tiny])

        row4 = sb("row4", [1, H], F32)
        with contextlib.ExitStack() as s0:
            ps_bc = s0.enter_context(nc.psum_tensor("ps_bc", [128, 512], F32))

            def bcast_row(src_ap, dst_ap, n, reads):
                p.op("pe", lambda e: e.matmul(ps_bc[:, 0:n], ones1f[:], src_ap, start=True, stop=True),
                     reads=[ones1f] + reads, writes=[ps_bc])
                p.op("act", lambda e: e.copy(dst_ap, ps_bc[:, 0:n]), reads=[ps_bc], writes=[dst_ap])

            bcast_row(tiny[:, 7:8], itemp_b[:], 1, [tiny])
            p.dma("dsp", tiny[:, 0:1], rs_h)
            bcast_row(tiny[:, 0:1], rs_b[:], 1, [tiny])
            p.dma("dsp", tiny[:, 1:2], rl_h)
            bcast_row(tiny[:, 1:2], rl_b[:], 1, [tiny])
            p.dma("dsp", row4[:], fb2_h)
            bcast_row(row4[:], b2g_b[:], H, [row4])
            p.dma("dsp", onorm_s[:], onorm_h)
        p.barrier()

        with contextlib.ExitStack() as s1:
            sb1 = lambda name, shape, dt: s1.enter_context(nc.sbuf_tensor(name, shape, dt))
            ps1 = lambda name, shape, dt: s1.enter_context(nc.psum_tensor(name, shape, dt))

            wq_l = sb1("wq_l", [128, 8, D], BF16)
            wk_l = sb1("wk_l", [128, 8, D], BF16)
            wv_l = sb1("wv_l", [128, 8, D], BF16)
            wb_l = sb1("wb_l", [128, 8, H], BF16)
            qcw = sb1("qcw", [128, 8, 4], F32)
            kcw = sb1("kcw", [128, 8, 4], F32)
            vcw = sb1("vcw", [128, 8, 4], F32)
            qcd = sb1("qcd", [128, 8 * 4 * 128], BF16)
            kcd = sb1("kcd", [128, 8 * 4 * 128], BF16)
            vcd = sb1("vcd", [128, 8 * 4 * 128], BF16)

            for w_h, w_l in ((wq_h, wq_l), (wk_h, wk_l), (wv_h, wv_l)):
                p.dma("dpool", w_l[:], w_h.rearrange("(k p) m -> p k m", p=128))
            p.dma("dpool", wb_l[:], wb_h.rearrange("(k p) m -> p k m", p=128))
            for c_h, c_w in ((qc_h, qcw), (kc_h, kcw), (vc_h, vcw)):
                p.dma("dsp", c_w[:], c_h.rearrange("(c p) k -> p c k", p=128))
            for cw, cd in ((qcw, qcd), (kcw, kcd), (vcw, vcd)):
                for m in range(8):
                    for k in range(4):
                        o = (m * 4 + k) * 128
                        p.op("dve", lambda e, cw=cw, cd=cd, m=m, k=k, o=o:
                             e.tensor_scalar(cd[:, o:o + 128], identb[:], cw[:, m, 3 - k:4 - k], None, ALU.mult, ALU.bypass),
                             reads=[identb, cw], writes=[cd])

            if "s_dbg7" in debug_taps:
                p.dma("dpool", s_dbg7.rearrange("(k p) m -> p k m", p=128), wq_l[:])
            z64 = sb1("z64", [128, 128], BF16)
            p.op("pool", lambda e: e.memset(z64[:], 0.0), writes=[z64])
            for m in range(8):
                p.dma("dsp", s_vf[m, :, 0:128], z64[:])

            xb_tok = sb1("xb_tok", [128, 4, D], BF16)
            xb_f = sb1("xb_f", [128, 8, S1], BF16)
            q_s = sb1("q_s", [128, 8, S1 + 3], BF16)
            k_s = sb1("k_s", [128, 8, S1 + 3], BF16)
            v_s = sb1("v_s", [128, 8, S1 + 3], BF16)
            q_park = sb1("q_park", [128, 8, S1], BF16)
            k_park = sb1("k_park", [128, 8, S1], BF16)
            v_park = sb1("v_park", [128, 8, S1], BF16)
            v_f32 = sb1("v_f32", [128, S1], F32)
            v_tok = sb1("v_tok", [128, 4, D], F32)
            blog_s = sb1("blog_s", [H, S1], F32)

            pA = ps1("pA", [128, 1024], BF16)
            pB = ps1("pB", [128, 512], F32)
            pB2 = ps1("pB2", [128, 512], F32)
            pC = ps1("pC", [128, 512], F32)
            pC2 = ps1("pC2", [128, 512], F32)
            pD = ps1("pD", [4, 512], F32)
            pE = ps1("pE", [128, 512], F32)

            for m in range(8):
                p.op("pool", lambda e, m=m: e.memset(q_s[:, m, 0:3], 0.0), writes=[q_s])
                p.op("pool", lambda e, m=m: e.memset(k_s[:, m, 0:3], 0.0), writes=[k_s])
                p.op("pool", lambda e, m=m: e.memset(v_s[:, m, 0:3], 0.0), writes=[v_s])

            for s in range(NSEG1):
                t0 = s * S1
                p.dma("dpool", xb_tok[:], x_h[t0:t0 + S1, :].rearrange("(ti p) d -> p ti d", p=128))
                for ti in range(4):
                    for j in range(8):
                        p.op("pe", lambda e, ti=ti, j=j:
                             e.transpose(pA[:, j * 128:(j + 1) * 128], xb_tok[:, ti, j * 128:(j + 1) * 128], identb[:]),
                             reads=[xb_tok, identb], writes=[pA], inc=(j == 7))
                    for j in range(8):
                        p.op("act", lambda e, ti=ti, j=j:
                             e.copy(xb_f[:, j, ti * 128:(ti + 1) * 128], pA[:, j * 128:(j + 1) * 128]),
                             reads=[pA], writes=[xb_f], inc=(j == 7))
                for (w_l, cdt, dst_s, park, sname) in (
                        (wq_l, qcd, q_s, q_park, s_qf),
                        (wk_l, kcd, k_s, k_park, s_kf),
                        (wv_l, vcd, v_s, v_park, s_vf)):
                    isv = sname is s_vf
                    for m in range(8):
                        pp = pB if m % 2 == 0 else pB2
                        for k in range(8):
                            p.op("pe", lambda e, pp=pp, w_l=w_l, k=k, m=m:
                                 e.matmul(pp[:], w_l[:, k, m * 128:(m + 1) * 128], xb_f[:, k, :],
                                          start=(k == 0), stop=(k == 7)),
                                 reads=[w_l, xb_f], writes=[pp], inc=(k == 7))
                        p.op("act", lambda e, pp=pp, dst_s=dst_s, m=m:
                             e.copy(dst_s[:, m, 3:3 + S1], pp[:]), reads=[pp], writes=[dst_s])
                    for m in range(8):
                        pp = pC if m % 2 == 0 else pC2
                        for k in range(4):
                            p.op("pe", lambda e, pp=pp, cdt=cdt, dst_s=dst_s, m=m, k=k:
                                 e.matmul(pp[:], cdt[:, (m * 4 + k) * 128:(m * 4 + k + 1) * 128],
                                          dst_s[:, m, 3 - k:3 - k + S1], start=(k == 0), stop=(k == 3)),
                                 reads=[cdt, dst_s], writes=[pp], inc=(k == 3))
                        p.op("act", lambda e, pp=pp, park=park, m=m:
                             e.activation(park[:, m, :], pp[:], AF.Silu), reads=[pp], writes=[park])
                        if isv:
                            p.op("act", lambda e, pp=pp, m=m:
                                 e.activation(v_f32[:], pp[:], AF.Silu), reads=[pp], writes=[v_f32])
                            for ti in range(4):
                                p.op("pe", lambda e, ti=ti:
                                     e.transpose(pE[:, (ti % 4) * 128:(ti % 4) * 128 + 128],
                                                 v_f32[:, ti * 128:(ti + 1) * 128], identf[:]),
                                     reads=[v_f32, identf], writes=[pE], inc=(ti == 3))
                            for ti in range(4):
                                p.op("act", lambda e, ti=ti, m=m:
                                     e.copy(v_tok[:, ti, m * 128:(m + 1) * 128], pE[:, ti * 128:ti * 128 + 128]),
                                     reads=[pE], writes=[v_tok], inc=(ti == 3))
                    for m in range(8):
                        p.op("act", lambda e, dst_s=dst_s, m=m:
                             e.copy(dst_s[:, m, 0:3], dst_s[:, m, S1:S1 + 3]), reads=[dst_s], writes=[dst_s],
                             inc=(m == 7))
                    if isv:
                        p.dma("dact", sname[:, :, 128 + t0:128 + t0 + S1].rearrange("m p t -> p m t"),
                              park[:])
                    else:
                        p.dma("dact", sname[:, :, t0:t0 + S1].rearrange("m p t -> p m t"),
                              park[:])
                p.dma("dact", s_vtok[t0:t0 + S1, :].rearrange("(ti p) d -> p ti d", p=128),
                      v_tok[:])
                for k in range(8):
                    p.op("pe", lambda e, k=k: e.matmul(pD[:], wb_l[:, k, :], xb_f[:, k, :],
                                                       start=(k == 0), stop=(k == 7)),
                         reads=[wb_l, xb_f], writes=[pD], inc=(k == 7))
                p.op("act", lambda e: e.copy(blog_s[:], pD[:]), reads=[pD], writes=[blog_s])
                p.dma("dact", s_blog[:, t0:t0 + S1], blog_s[:])

        p.barrier()

        if max_phase < 2:
            p.barrier_dma_done()
            p.emit()
            return nc
        with contextlib.ExitStack() as s2:
            sb2 = lambda name, shape, dt: s2.enter_context(nc.sbuf_tensor(name, shape, dt))
            ps2_ = lambda name, shape, dt: s2.enter_context(nc.psum_tensor(name, shape, dt))

            blog_sb = sb2("blog_sb", [H, T], F32)
            btok = sb2("btok", [128, NCH * H], F32)
            nbtok = sb2("nbtok", [128, NCH * H], F32)
            S_f = sb2("S_f", [128, H * 2 * DH], F32)
            S_b = sb2("S_b", [128, H * 2 * DH], BF16)
            identb4 = sb2("identb4", [128, 512], BF16)
            qf_c = [sb2(f"qf_c{i}", [128, 8, CH], BF16) for i in range(2)]
            kf_c = [sb2(f"kf_c{i}", [128, 8, CH], BF16) for i in range(2)]
            vt_c = [sb2(f"vt_c{i}", [128, D], F32) for i in range(2)]
            dpark = [sb2(f"dpark{i}", [128, D], F32) for i in range(2)]
            qtokB = [sb2(f"qtok{i}", [128, D], F32) for i in range(2)]
            ktokB = [sb2(f"ktok{i}", [128, D], F32) for i in range(2)]
            qnB = [sb2(f"qn{i}", [128, D], BF16) for i in range(2)]
            knB = [sb2(f"kn{i}", [128, D], BF16) for i in range(2)]
            qnfB = [sb2(f"qnf{i}", [128, 8, CH], BF16) for i in range(2)]
            knfB = [sb2(f"knf{i}", [128, 8, CH], BF16) for i in range(2)]
            kbB = [sb2(f"kb{i}", [128, D], BF16) for i in range(2)]
            vbB = [sb2(f"vb{i}", [128, D], BF16) for i in range(2)]
            HmB = [sb2(f"Hm{i}", [128, 512], BF16) for i in range(2)]
            GmB = [sb2(f"Gm{i}", [128, 512], BF16) for i in range(2)]
            EmB = [sb2(f"Em{i}", [128, 512], BF16) for i in range(2)]
            AttTB = [sb2(f"AttT{i}", [128, 512], BF16) for i in range(2)]
            usbB = [sb2(f"usb{i}", [128, D], F32) for i in range(2)]
            wfB = [sb2(f"wf{i}", [128, 8, CH], BF16) for i in range(2)]
            uaB = [sb2(f"ua{i}", [128, D], F32) for i in range(2)]
            uabB = [sb2(f"uab{i}", [128, D], BF16) for i in range(2)]
            ssB = [sb2(f"ss{i}", [128, 24], F32) for i in range(2)]
            sqscrP = [sb2(f"sqscrP{i}", [128, DH], BF16) for i in range(2)]

            pT1 = ps2_("pT1", [128, 1024], BF16)
            pT2 = ps2_("pT2", [128, 1024], BF16)
            pP3 = ps2_("pP3", [128, 512], F32)
            pP4 = ps2_("pP4", [128, 512], F32)
            pP5 = ps2_("pP5", [128, 512], F32)
            pS1 = ps2_("pS1", [128, 512], F32)
            pS2b = ps2_("pS2b", [128, 512], F32)
            pS3 = ps2_("pS3", [128, 512], F32)

            p.dma("dsp", blog_sb[:], s_blog)
            p.op("act", lambda e: e.activation(blog_sb[:], blog_sb[:], AF.Sigmoid),
                 reads=[blog_sb], writes=[blog_sb])
            for c in range(NCH):
                p.op("pe", lambda e, c=c: e.transpose(pP3[:, 0:H], blog_sb[0:H, c * CH:(c + 1) * CH],
                                                      identf[0:H, 0:H]),
                     reads=[blog_sb, identf], writes=[pP3])
                p.op("act", lambda e, c=c: e.copy(btok[:, c * H:(c + 1) * H], pP3[:, 0:H]),
                     reads=[pP3], writes=[btok])
            p.op("dve", lambda e: e.tensor_scalar(nbtok[:], btok[:], -1.0, None, ALU.mult, ALU.bypass),
                 reads=[btok], writes=[nbtok])
            p.op("pool", lambda e: e.memset(S_f[:], 0.0), writes=[S_f])
            p.op("pool", lambda e: e.memset(S_b[:], 0.0), writes=[S_b])
            for h in range(H):
                p.op("dve", lambda e, h=h: e.tensor_copy(identb4[:, h * 128:(h + 1) * 128], identb[:]),
                     reads=[identb], writes=[identb4])

            NFAC = 7

            def p2_prep(c, qf, kf, vt, qtok, ktok, qn, kn, qnf, knf, kb, vb,
                        Hm, Gm, Em, AttT, usb, wf, ss, sqp):
                t0 = c * CH
                p.dma("dsp", qf[:], s_qf[:, :, t0:t0 + CH].rearrange("m p t -> p m t"))
                p.dma("dsp", kf[:], s_kf[:, :, t0:t0 + CH].rearrange("m p t -> p m t"))
                p.dma("dsp", vt[:], s_vtok[t0:t0 + CH, :])
                for h in range(H):
                    for j in range(2):
                        p.op("pe", lambda e, h=h, j=j, qf=qf:
                             e.transpose(pT1[:, h * DH + j * CH:h * DH + (j + 1) * CH], qf[:, 2 * h + j, :], identb[:]),
                             reads=[qf, identb], writes=[pT1], inc=(h == H - 1 and j == 1))
                for h in range(H):
                    for j in range(2):
                        p.op("pe", lambda e, h=h, j=j, kf=kf:
                             e.transpose(pT2[:, h * DH + j * CH:h * DH + (j + 1) * CH], kf[:, 2 * h + j, :], identb[:]),
                             reads=[kf, identb], writes=[pT2], inc=(h == H - 1 and j == 1))
                p.op("act", lambda e, qtok=qtok: e.copy(qtok[:], pT1[:, 0:D]), reads=[pT1], writes=[qtok])
                p.op("dve", lambda e, ktok=ktok: e.tensor_copy(ktok[:], pT2[:, 0:D]), reads=[pT2], writes=[ktok])
                for h in range(H):
                    p.op("dve", lambda e, ss=ss, h=h: e.memset(ss[:, h:h + 1], 0.0), writes=[ss])
                    p.op("act", lambda e, qtok=qtok, ss=ss, h=h, sqp=sqp:
                         e.activation(sqp[:], qtok[:, h * DH:(h + 1) * DH], AF.Square, accum_out=ss[:, h:h + 1]),
                         reads=[qtok], writes=[sqp, ss])
                    p.op("dve", lambda e, ss=ss, h=h: e.memset(ss[:, 4 + h:5 + h], 0.0), writes=[ss])
                    p.op("act", lambda e, ktok=ktok, ss=ss, h=h, sqp=sqp:
                         e.activation(sqp[:], ktok[:, h * DH:(h + 1) * DH], AF.Square, accum_out=ss[:, 4 + h:5 + h]),
                         reads=[ktok], writes=[sqp, ss])
                p.op("act", lambda e, ss=ss: e.activation(ss[:, 8:16], ss[:, 0:8], AF.Ln),
                     reads=[ss], writes=[ss])
                p.op("act", lambda e, ss=ss: e.activation(ss[:, 16:24], ss[:, 8:16], AF.Exp, scale=-0.5),
                     reads=[ss], writes=[ss])
                for h in range(H):
                    bcol = btok[:, c * H + h:c * H + h + 1]
                    p.op("dve", lambda e, qn=qn, qtok=qtok, ss=ss, h=h:
                         e.tensor_single_scalar(qn[:, h * DH:(h + 1) * DH], qtok[:, h * DH:(h + 1) * DH],
                                                ss[:, 16 + h:17 + h], ALU.mult),
                         reads=[qtok, ss], writes=[qn])
                    p.op("dve", lambda e, kn=kn, ktok=ktok, ss=ss, h=h:
                         e.tensor_single_scalar(kn[:, h * DH:(h + 1) * DH], ktok[:, h * DH:(h + 1) * DH],
                                                ss[:, 20 + h:21 + h], ALU.mult),
                         reads=[ktok, ss], writes=[kn])
                    p.op("dve", lambda e, kb=kb, kn=kn, bcol=bcol, h=h:
                         e.tensor_single_scalar(kb[:, h * DH:(h + 1) * DH], kn[:, h * DH:(h + 1) * DH], bcol, ALU.mult),
                         reads=[kn, btok], writes=[kb])
                    p.op("dve", lambda e, vb=vb, vt=vt, bcol=bcol, h=h:
                         e.tensor_single_scalar(vb[:, h * DH:(h + 1) * DH], vt[:, h * DH:(h + 1) * DH], bcol, ALU.mult),
                         reads=[vt, btok], writes=[vb])
                for h in range(H):
                    for j in range(2):
                        p.op("pe", lambda e, qn=qn, h=h, j=j:
                             e.transpose(pT1[:, h * DH + j * CH:h * DH + (j + 1) * CH],
                                         qn[:, h * DH + j * CH:h * DH + (j + 1) * CH], identb[:]),
                             reads=[qn, identb], writes=[pT1], inc=(h == H - 1 and j == 1))
                for h in range(H):
                    for j in range(2):
                        p.op("pe", lambda e, kn=kn, h=h, j=j:
                             e.transpose(pT2[:, h * DH + j * CH:h * DH + (j + 1) * CH],
                                         kn[:, h * DH + j * CH:h * DH + (j + 1) * CH], identb[:]),
                             reads=[kn, identb], writes=[pT2], inc=(h == H - 1 and j == 1))
                p.op("act", lambda e, qnf=qnf: e.copy(qnf[:].rearrange("p m t -> p (m t)"), pT1[:, 0:D]),
                     reads=[pT1], writes=[qnf])
                p.op("act", lambda e, knf=knf: e.copy(knf[:].rearrange("p m t -> p (m t)"), pT2[:, 0:D]),
                     reads=[pT2], writes=[knf])
                for h in range(H):
                    for j in range(2):
                        p.op("pe", lambda e, knf=knf, h=h, j=j:
                             e.matmul(pP3[:, h * CH:(h + 1) * CH], knf[:, 2 * h + j, :], knf[:, 2 * h + j, :],
                                      start=(j == 0), stop=(j == 1)),
                             reads=[knf], writes=[pP3], inc=(h == H - 1 and j == 1))
                for h in range(H):
                    p.op("dve", lambda e, Hm=Hm, h=h, c=c:
                         e.scalar_tensor_tensor(Hm[:, h * CH:(h + 1) * CH], pP3[:, h * CH:(h + 1) * CH],
                                                nbtok[:, c * H + h:c * H + h + 1], mask_lt[:], ALU.mult, ALU.mult),
                         reads=[pP3, nbtok, mask_lt], writes=[Hm], inc=(h == H - 1))
                for h in range(H):
                    for j in range(2):
                        p.op("pe", lambda e, knf=knf, qnf=qnf, h=h, j=j:
                             e.matmul(pP4[:, h * CH:(h + 1) * CH], knf[:, 2 * h + j, :], qnf[:, 2 * h + j, :],
                                      start=(j == 0), stop=(j == 1)),
                             reads=[knf, qnf], writes=[pP4], inc=(h == H - 1 and j == 1))
                for h in range(H):
                    p.op("dve", lambda e, AttT=AttT, h=h:
                         e.tensor_mul(AttT[:, h * CH:(h + 1) * CH], pP4[:, h * CH:(h + 1) * CH], mask_ut[:]),
                         reads=[pP4, mask_ut], writes=[AttT], inc=(h == H - 1))
                for h in range(H):
                    p.op("pe", lambda e, Hm=Hm, h=h:
                         e.transpose(pT2[:, h * CH:(h + 1) * CH], Hm[:, h * CH:(h + 1) * CH], identb[:]),
                         reads=[Hm, identb], writes=[pT2], inc=(h == H - 1))
                p.op("act", lambda e, Gm=Gm: e.copy(Gm[:], pT2[:, 0:512]), reads=[pT2], writes=[Gm])
                p.op("dve", lambda e, Em=Em, Gm=Gm: e.tensor_add(Em[:], Gm[:], identb4[:]),
                     reads=[Gm, identb4], writes=[Em])
                for jj in range(1, NFAC):
                    for h in range(H):
                        p.op("pe", lambda e, Gm=Gm, Hm=Hm, h=h:
                             e.matmul(pP3[:, h * CH:(h + 1) * CH], Gm[:, h * CH:(h + 1) * CH],
                                      Hm[:, h * CH:(h + 1) * CH], start=True, stop=True),
                             reads=[Gm, Hm], writes=[pP3], inc=(h == H - 1))
                    if jj < NFAC - 1:
                        for h in range(H):
                            p.op("pe", lambda e, Gm=Gm, Hm=Hm, h=h:
                                 e.matmul(pP4[:, h * CH:(h + 1) * CH], Hm[:, h * CH:(h + 1) * CH],
                                          Gm[:, h * CH:(h + 1) * CH], start=True, stop=True),
                                 reads=[Hm, Gm], writes=[pP4], inc=(h == H - 1))
                        p.op("dve", lambda e, Gm=Gm: e.tensor_copy(Gm[:], pP4[:, 0:512]),
                             reads=[pP4], writes=[Gm])
                    p.op("act", lambda e, Hm=Hm: e.copy(Hm[:], pP3[:, 0:512]), reads=[pP3], writes=[Hm])
                    for h in range(H):
                        p.op("pe", lambda e, Em=Em, h=h:
                             e.matmul(pP5[:, h * CH:(h + 1) * CH], identb[:], Em[:, h * CH:(h + 1) * CH],
                                      start=True, stop=False),
                             reads=[identb, Em], writes=[pP5], inc=False)
                        p.op("pe", lambda e, Em=Em, Hm=Hm, h=h:
                             e.matmul(pP5[:, h * CH:(h + 1) * CH], Hm[:, h * CH:(h + 1) * CH],
                                      Em[:, h * CH:(h + 1) * CH], start=False, stop=True),
                             reads=[Hm, Em], writes=[pP5], inc=(h == H - 1))
                    p.op("dve", lambda e, Em=Em: e.tensor_copy(Em[:], pP5[:, 0:512]), reads=[pP5], writes=[Em])
                for h in range(H):
                    pb = pP3 if h < 2 else pP4
                    p.op("pe", lambda e, Em=Em, vb=vb, pb=pb, h=h:
                         e.matmul(pb[:, (h % 2) * DH:(h % 2 + 1) * DH], Em[:, h * CH:(h + 1) * CH],
                                  vb[:, h * DH:(h + 1) * DH], start=True, stop=True),
                         reads=[Em, vb], writes=[pb], inc=(h % 2 == 1))
                p.op("dve", lambda e, usb=usb: e.tensor_copy(usb[:, 0:512], pP3[:, 0:512]),
                     reads=[pP3], writes=[usb])
                p.op("dve", lambda e, usb=usb: e.tensor_copy(usb[:, 512:1024], pP4[:, 0:512]),
                     reads=[pP4], writes=[usb])
                for h in range(H):
                    pb = pP3 if h < 2 else pP4
                    p.op("pe", lambda e, Em=Em, kb=kb, pb=pb, h=h:
                         e.matmul(pb[:, (h % 2) * DH:(h % 2 + 1) * DH], Em[:, h * CH:(h + 1) * CH],
                                  kb[:, h * DH:(h + 1) * DH], start=True, stop=True),
                         reads=[Em, kb], writes=[pb], inc=(h % 2 == 1))
                p.op("act", lambda e, qn=qn: e.copy(qn[:, 0:512], pP3[:, 0:512]), reads=[pP3], writes=[qn])
                p.op("act", lambda e, qn=qn: e.copy(qn[:, 512:1024], pP4[:, 0:512]), reads=[pP4], writes=[qn])
                for h in range(H):
                    for j in range(2):
                        p.op("pe", lambda e, qn=qn, h=h, j=j:
                             e.transpose(pT1[:, h * DH + j * CH:h * DH + (j + 1) * CH],
                                         qn[:, h * DH + j * CH:h * DH + (j + 1) * CH], identb[:]),
                             reads=[qn, identb], writes=[pT1], inc=(h == H - 1 and j == 1))
                p.op("act", lambda e, wf=wf: e.copy(wf[:].rearrange("p m t -> p (m t)"), pT1[:, 0:D]),
                     reads=[pT1], writes=[wf])

            def p2_scan(c, pc):
                t0 = c * CH
                qnf, knf, kn, AttT, usb, wf = qnfB[pc], knfB[pc], knB[pc], AttTB[pc], usbB[pc], wfB[pc]
                ua, uab = uaB[pc], uabB[pc]
                for h in range(H):
                    pb = pS1 if h < 2 else pS2b
                    for j in range(2):
                        p.op("pe", lambda e, wf=wf, pb=pb, h=h, j=j:
                             e.matmul(pb[:, (h % 2) * DH:(h % 2 + 1) * DH], wf[:, 2 * h + j, :],
                                      S_b[:, (h * 2 + j) * DH:(h * 2 + j + 1) * DH],
                                      start=(j == 0), stop=(j == 1)),
                             reads=[wf, S_b], writes=[pb], inc=(h % 2 == 1 and j == 1))
                p.op("dve", lambda e, ua=ua, usb=usb: e.tensor_sub(ua[:, 0:512], usb[:, 0:512], pS1[:, 0:512]),
                     reads=[usb, pS1], writes=[ua])
                p.op("dve", lambda e, ua=ua, usb=usb: e.tensor_sub(ua[:, 512:1024], usb[:, 512:1024], pS2b[:, 0:512]),
                     reads=[usb, pS2b], writes=[ua])
                p.op("act", lambda e, uab=uab, ua=ua: e.copy(uab[:], ua[:]), reads=[ua], writes=[uab])
                for h in range(H):
                    pb = pS1 if h < 2 else pS2b
                    sl = slice((h % 2) * DH, (h % 2 + 1) * DH)
                    for j in range(2):
                        p.op("pe", lambda e, qnf=qnf, pb=pb, sl=sl, h=h, j=j:
                             e.matmul(pb[:, sl], qnf[:, 2 * h + j, :],
                                      S_b[:, (h * 2 + j) * DH:(h * 2 + j + 1) * DH], start=(j == 0), stop=False),
                             reads=[qnf, S_b], writes=[pb], inc=False)
                    p.op("pe", lambda e, AttT=AttT, uab=uab, pb=pb, sl=sl, h=h:
                         e.matmul(pb[:, sl], AttT[:, h * CH:(h + 1) * CH], uab[:, h * DH:(h + 1) * DH],
                                  start=False, stop=True),
                         reads=[AttT, uab], writes=[pb], inc=(h % 2 == 1))
                p.op("dve", lambda e, pc=pc: e.tensor_copy(dpark[pc][:, 0:512], pS1[:, 0:512]),
                     reads=[pS1], writes=[dpark[pc]])
                p.op("dve", lambda e, pc=pc: e.tensor_copy(dpark[pc][:, 512:1024], pS2b[:, 0:512]),
                     reads=[pS2b], writes=[dpark[pc]])
                p.dma("dact", s_delta[t0:t0 + CH, :], dpark[pc][:])
                for wave in range(2):
                    for hh in range(2):
                        h = wave * 2 + hh
                        pb = pS1 if hh == 0 else pS2b
                        for j in range(2):
                            p.op("pe", lambda e, kn=kn, uab=uab, pb=pb, h=h, j=j:
                                 e.matmul(pb[:, j * DH:(j + 1) * DH], kn[:, h * DH + j * CH:h * DH + (j + 1) * CH],
                                          uab[:, h * DH:(h + 1) * DH], start=True, stop=True),
                                 reads=[kn, uab], writes=[pb], inc=(j == 1))
                    for hh in range(2):
                        h = wave * 2 + hh
                        pb = pS1 if hh == 0 else pS2b
                        p.op("dve", lambda e, pb=pb, h=h:
                             e.tensor_add(S_f[:, h * 2 * DH:(h + 1) * 2 * DH], S_f[:, h * 2 * DH:(h + 1) * 2 * DH],
                                          pb[:, 0:512]),
                             reads=[S_f, pb], writes=[S_f])
                    p.op("act", lambda e, wave=wave:
                         e.copy(S_b[:, wave * 1024:(wave + 1) * 1024], S_f[:, wave * 1024:(wave + 1) * 1024]),
                         reads=[S_f], writes=[S_b])

            def do_prep(c):
                pc = c % 2
                p2_prep(c, qf_c[pc], kf_c[pc], vt_c[pc], qtokB[pc], ktokB[pc], qnB[pc], knB[pc],
                        qnfB[pc], knfB[pc], kbB[pc], vbB[pc], HmB[pc], GmB[pc], EmB[pc], AttTB[pc],
                        usbB[pc], wfB[pc], ssB[pc], sqscrP[pc])

            do_prep(0)
            for c in range(NCH):
                if c + 1 < NCH:
                    do_prep(c + 1)
                p2_scan(c, c % 2)

        p.barrier()

        if max_phase < 3:
            p.barrier_dma_done()
            p.emit()
            return nc
        with contextlib.ExitStack() as s3:
            sb3 = lambda name, shape, dt: s3.enter_context(nc.sbuf_tensor(name, shape, dt))
            ps3_ = lambda name, shape, dt: s3.enter_context(nc.psum_tensor(name, shape, dt))

            firLd = sb3("firLd", [128, 16, 64], F32)
            firSd = sb3("firSd", [128, 16, 5], F32)
            firL_lhs = sb3("firL_lhs", [128, 16 * 16 * 128], BF16)
            firS_lhs = sb3("firS_lhs", [128, 16 * 2 * 128], BF16)

            for g in range(16):
                ct, half = g // 2, (g % 2) * 64
                p.dma("dsp", firLd[0:64, g, :], firL_h[ct * 128 + half:ct * 128 + half + 64, :])
                p.dma("dsp", firLd[64:128, g, :], firL_h[ct * 128 + half:ct * 128 + half + 64, :])
                p.dma("dsp", firSd[0:64, g, :], firS_h[ct * 128 + half:ct * 128 + half + 64, :])
                p.dma("dsp", firSd[64:128, g, :], firS_h[ct * 128 + half:ct * 128 + half + 64, :])
            p.op("pool", lambda e: e.memset(firS_lhs[:], 0.0), writes=[firS_lhs])
            for g in range(16):
                for bb in range(16):
                    o = (g * 16 + bb) * 128
                    for sg in range(2):
                        for ss in range(2):
                            kk = 4 * bb + 2 * sg + ss
                            p.op("dve", lambda e, o=o, sg=sg, ss=ss, g=g, kk=kk:
                                 e.tensor_scalar(firL_lhs[ss * 64:(ss + 1) * 64, o + sg * 64:o + sg * 64 + 64],
                                                 did2[ss * 64:(ss + 1) * 64, :],
                                                 firLd[ss * 64:(ss + 1) * 64, g, 63 - kk:64 - kk],
                                                 None, ALU.mult, ALU.bypass),
                                 reads=[did2, firLd], writes=[firL_lhs], inc=(sg == 1 and ss == 1))
                for bb in range(2):
                    o = (g * 2 + bb) * 128
                    for sg in range(2):
                        for ss in range(2):
                            kk = 4 * bb + 2 * sg + ss
                            if kk >= 5:
                                continue
                            p.op("dve", lambda e, o=o, sg=sg, ss=ss, g=g, kk=kk:
                                 e.tensor_scalar(firS_lhs[ss * 64:(ss + 1) * 64, o + sg * 64:o + sg * 64 + 64],
                                                 did2[ss * 64:(ss + 1) * 64, :],
                                                 firSd[ss * 64:(ss + 1) * 64, g, 4 - kk:5 - kk],
                                                 None, ALU.mult, ALU.bypass),
                                 reads=[did2, firSd], writes=[firS_lhs], inc=True)

            HLO = 66
            vdup = [sb3(f"vdup{i}", [128, HLO + S3], BF16) for i in range(4)]
            scrL = [sb3(f"scrL{i}", [128, S3 + 2], F32) for i in range(2)]
            scrS = [sb3(f"scrS{i}", [128, S3 + 2], F32) for i in range(2)]
            ls_tok = sb3("ls_tok", [128, 2, D], F32)
            ll_tok = sb3("ll_tok", [128, 2, D], F32)

            pF = [ps3_(f"pF{i}", [128, 512], F32) for i in range(2)]
            pFS = [ps3_(f"pFS{i}", [128, 512], F32) for i in range(2)]
            pY0 = ps3_("pY0", [128, 512], F32)
            pY1 = ps3_("pY1", [128, 512], F32)
            cmb_scr = sb3("cmb_scr", [128, 256], F32)

            def fir_combine(scrpair, dst_t, ct):
                for ti in range(2):
                    for gg in range(2):
                        o = ti * 128 + gg * 64
                        sc = scrpair[gg]
                        p.op("pe", lambda e, sc=sc, o=o, ti=ti:
                             e.matmul(pY0[:, o:o + 64], sc[0:64, 2 + ti * 128:2 + ti * 128 + 128],
                                      identf[0:64, 0:64], start=True, stop=True),
                             reads=[sc, identf], writes=[pY0], inc=(gg == 1))
                        p.op("pe", lambda e, sc=sc, o=o, ti=ti:
                             e.matmul(pY1[:, o:o + 64], sc[64:128, ti * 128:ti * 128 + 128],
                                      identf[64:128, 64:128], start=True, stop=True),
                             reads=[sc, identf], writes=[pY1], inc=(gg == 1))
                p.op("act", lambda e: e.copy(cmb_scr[:], pY0[:, 0:256]),
                     reads=[pY0], writes=[cmb_scr])
                for ti in range(2):
                    p.op("dve", lambda e, dst_t=dst_t, ct=ct, ti=ti:
                         e.tensor_add(dst_t[:, ti, ct * 128:(ct + 1) * 128],
                                      cmb_scr[:, ti * 128:(ti + 1) * 128], pY1[:, ti * 128:(ti + 1) * 128]),
                         reads=[cmb_scr, pY1], writes=[dst_t])

            for s in range(p3a_segs):
                t0 = s * S3
                for ct in range(8):
                    for gg in range(2):
                        g = ct * 2 + gg
                        half = gg * 64
                        vd = vdup[(ct % 2) * 2 + gg]
                        vq = "dsp" if gg == 0 else "dpool"
                        p.dma(vq, vd[0:64, :], s_vf[ct, half:half + 64, 128 + t0 - HLO:128 + t0 + S3])
                        p.dma(vq, vd[64:128, 1:HLO + S3],
                              s_vf[ct, half:half + 64, 128 + t0 - HLO:128 + t0 + S3 - 1])
                        if p3a_stage < 2:
                            continue
                        for bb in range(16):
                            p.op("pe", lambda e, vd=vd, g=g, bb=bb, gg=gg:
                                 e.matmul(pF[gg][:, 0:S3 + 2], firL_lhs[:, (g * 16 + bb) * 128:(g * 16 + bb + 1) * 128],
                                          vd[:, HLO - 4 * bb - 2:HLO - 4 * bb + S3], start=(bb == 0), stop=(bb == 15)),
                                 reads=[firL_lhs, vd], writes=[pF[gg]], inc=(bb == 15))
                        for bb in range(2):
                            p.op("pe", lambda e, vd=vd, g=g, bb=bb, gg=gg:
                                 e.matmul(pFS[gg][:, 0:S3 + 2], firS_lhs[:, (g * 2 + bb) * 128:(g * 2 + bb + 1) * 128],
                                          vd[:, HLO - 4 * bb - 2:HLO - 4 * bb + S3], start=(bb == 0), stop=(bb == 1)),
                                 reads=[firS_lhs, vd], writes=[pFS[gg]], inc=(bb == 1))
                        p.op("act", lambda e, gg=gg: e.copy(scrL[gg][:], pF[gg][:, 0:S3 + 2]),
                             reads=[pF[gg]], writes=[scrL[gg]])
                        p.op("act", lambda e, gg=gg: e.copy(scrS[gg][:], pFS[gg][:, 0:S3 + 2]),
                             reads=[pFS[gg]], writes=[scrS[gg]])
                    if p3a_stage < 3:
                        continue
                    fir_combine(scrL, ll_tok, ct)
                    fir_combine(scrS, ls_tok, ct)
                p.dma("dact", s_ls[t0:t0 + S3, :].rearrange("(ti p) d -> p ti d", p=128), ls_tok[:])
                p.dma("dact", s_ll[t0:t0 + S3, :].rearrange("(ti p) d -> p ti d", p=128), ll_tok[:])

        p.barrier()
        if max_phase < 3.5:
            p.barrier_dma_done()
            p.emit()
            return nc
        with contextlib.ExitStack() as s3:
            sb3 = lambda name, shape, dt: s3.enter_context(nc.sbuf_tensor(name, shape, dt))
            ps3_ = lambda name, shape, dt: s3.enter_context(nc.psum_tensor(name, shape, dt))

            wo_l = sb3("wo_l", [128, 8, D], BF16)
            w1x_l = sb3("w1x_l", [128, 8, D], BF16)
            w1s_l = sb3("w1s_l", [16, D], BF16)
            w2g_l = sb3("w2g_l", [128, 8, H], BF16)
            fb1v = sb3("fb1v", [128, 8], F32)
            if p3b_weights:
                with contextlib.ExitStack() as s3w:
                    wo_raw = s3w.enter_context(nc.sbuf_tensor("wo_raw", [128, 8, D], BF16))
                    p.dma("dpool", wo_raw[:], wo_h.rearrange("(k p) m -> p k m", p=128))
                    for j in range(8):
                        if skip_fold:
                            p.op("dve", lambda e, j=j: e.tensor_copy(wo_l[:, j, :], wo_raw[:, j, :]),
                                 reads=[wo_raw], writes=[wo_l])
                        else:
                            p.op("dve", lambda e, j=j: e.tensor_single_scalar(wo_l[:, j, :], wo_raw[:, j, :],
                                                                              onorm_s[:, j % 2:j % 2 + 1], ALU.mult),
                                 reads=[wo_raw, onorm_s], writes=[wo_l])
                p.barrier()
                p.dma("dpool", w1x_l[:], w1x_h.rearrange("(k p) m -> p k m", p=128))
                p.dma("dpool", w1s_l[:], w1s_h)
                p.dma("dpool", w2g_l[:], w2g_h.rearrange("(k p) m -> p k m", p=128))
                p.dma("dsp", fb1v[:], fb1_h)

            ls_tokB = [sb3(f"ls_tok3b{i}", [128, 2, D], F32) for i in range(2)]
            ll_tokB = [sb3(f"ll_tok3b{i}", [128, 2, D], F32) for i in range(2)]
            dl_tokB = [sb3(f"dl_tok{i}", [128, 2, D], F32) for i in range(2)]
            vv_tokB = [sb3(f"vv_tok{i}", [128, 2, D], F32) for i in range(2)]
            x_tok3B = [sb3(f"x_tok3{i}", [128, 2, D], F32) for i in range(3)]
            xb_tok3B = [sb3(f"xb_tok3{i}", [128, 2, D], BF16) for i in range(2)]
            xb_f3B = [sb3(f"xb_f3{i}", [128, 8, S3], BF16) for i in range(2)]
            sums3B = [sb3(f"sums3{i}", [128, 2, H, 12], F32) for i in range(2)]
            stats3B = [sb3(f"stats3{i}", [128, 2, H, 16], F32) for i in range(2)]
            statsTB = [sb3(f"statsT{i}", [16, H, S3], BF16) for i in range(2)]
            sqscr3B = [sb3(f"sqscr3{i}", [128, D], F32) for i in range(2)]
            sqBB = [sb3(f"sqB{i}", [128, D], F32) for i in range(2)]
            sumsAbB = [sb3(f"sumsAb{i}", [128, 2, H, 4], F32) for i in range(2)]
            rms_scB = [sb3(f"rms_sc{i}", [128, 2, H, 4], F32) for i in range(2)]
            ln_sc = sb3("ln_sc", [128, 2, 8], F32)
            tmp3bB = [sb3(f"tmp3b{i}", [128, 2, H, 4], F32) for i in range(2)]
            hdnB = [sb3(f"hdn{i}", [128, 8, S3], BF16) for i in range(2)]
            lg3B = [sb3(f"lg3{i}", [128, 2, H, 4], F32) for i in range(2)]
            coefB = [sb3(f"coef{i}", [128, 2, H, 4], F32) for i in range(2)]
            sc3B = [sb3(f"sc3{i}", [128, 3, H, 4], F32) for i in range(2)]
            o_tok_s = [sb3(f"o_at{t}", [128, D], F32) for t in range(2)]
            o_tokB = [o_tok_s, o_tok_s]
            o_nbB = [sb3(f"o_nb{i}", [128, 2, D], BF16) for i in range(2)]
            o_fB = [sb3(f"o_f{i}", [128, 8, S3], BF16) for i in range(2)]
            xa_s = sb3("xa", [128, 2, D], F32)
            xaB = [xa_s, xa_s]
            x2pB = [xa_s, xa_s]

            pTT = ps3_("pTT3b", [128, 512], F32)
            pTb = ps3_("pTb", [128, 1024], BF16)
            pTb2 = ps3_("pTb2", [128, 1024], BF16)
            pG3 = ps3_("pG3", [128, 512], F32)
            pL3 = ps3_("pL3", [128, 512], F32)
            pZ3 = ps3_("pZ3", [128, 512], F32)
            pO3 = ps3_("pO3", [128, 512], F32)
            pST = ps3_("pST", [128, 512], F32)

            b2g_it = sb3("b2g_it", [128, H, 4], F32)
            for h in range(H):
                p.op("dve", lambda e, h=h:
                     e.tensor_single_scalar(b2g_it[:, h, :], b2g_b[:, 0:4], itemp_b[:, 0:1], ALU.mult),
                     reads=[b2g_b, itemp_b], writes=[b2g_it])
            if "s_dbg7" in debug_taps:
                p.dma("dpool", s_dbg7.rearrange("(k p) m -> p k m", p=128), wo_l[:])
            def p3b_segA(s, t0, ls_tok, ll_tok, dl_tok, vv_tok, x_tok3, xb_tok3, xb_f3,
                         sums3, stats3, statsT, sqscr3, tmp3b, sumsAb, rms_sc, hdn, lg3, coef, sc3,
                         o_tok, o_nb, o_f, xa, x2p):
                p.dma("dsp", ls_tok[:], s_ls[t0:t0 + S3, :].rearrange("(ti p) d -> p ti d", p=128))
                p.dma("dsp", ll_tok[:], s_ll[t0:t0 + S3, :].rearrange("(ti p) d -> p ti d", p=128))
                p.dma("dact", dl_tok[:], s_delta[t0:t0 + S3, :].rearrange("(ti p) d -> p ti d", p=128))
                p.dma("dact", vv_tok[:], s_vtok[t0:t0 + S3, :].rearrange("(ti p) d -> p ti d", p=128))
                p.dma("dpool", x_tok3[:], x_h[t0:t0 + S3, :].rearrange("(ti p) d -> p ti d", p=128))
                p.dma("dpool", xb_tok3[:], x_h[t0:t0 + S3, :].rearrange("(ti p) d -> p ti d", p=128))
                for ti in range(2):
                    for j in range(8):
                        p.op("pe", lambda e, xb_tok3=xb_tok3, ti=ti, j=j:
                             e.transpose(pTb[:, j * 128:(j + 1) * 128], xb_tok3[:, ti, j * 128:(j + 1) * 128], identb[:]),
                             reads=[xb_tok3, identb], writes=[pTb], inc=(j == 7))
                    p.op("act", lambda e, xb_f3=xb_f3, ti=ti:
                         e.copy(xb_f3[:, :, ti * 128:(ti + 1) * 128], pTb[:, 0:1024].rearrange("p (j c) -> p j c", j=8)),
                         reads=[pTb], writes=[xb_f3])
                comps = (ls_tok, ll_tok, dl_tok, vv_tok)
                for ti in range(2):
                    for ci, comp in enumerate(comps):
                        sqw = sqscr3B[(ti * 4 + ci) % 2]
                        p.op("dve", lambda e, comp=comp, ti=ti, ci=ci:
                             e.reduce_sum(sums3[:, ti, :, ci:ci + 1],
                                          comp[:, ti, :].rearrange("p (h d) -> p h d", h=H), axis=AX.X),
                             reads=[comp], writes=[sums3])
                        p.op("dve", lambda e, comp=comp, ti=ti, ci=ci:
                             e.reduce_sum(sumsAb[:, ti, :, ci:ci + 1],
                                          comp[:, ti, :].rearrange("p (h d) -> p h d", h=H), axis=AX.X,
                                          apply_absolute_value=True),
                             reads=[comp], writes=[sumsAb])
                        p.op("act", lambda e, comp=comp, sqw=sqw, ti=ti, ci=ci:
                             e.activation(sqw[:, 0:D], comp[:, ti, :], AF.Square),
                             reads=[comp], writes=[sqw])
                        p.op("dve", lambda e, sqw=sqw, ti=ti, ci=ci:
                             e.reduce_sum(sums3[:, ti, :, 4 + ci:5 + ci],
                                          sqw[:, 0:D].rearrange("p (h d) -> p h d", h=H), axis=AX.X),
                             reads=[sqw], writes=[sums3])
                    p.op("dve", lambda e, ti=ti:
                         e.tensor_scalar(stats3[:, ti, :, 0:4], sums3[:, ti, :, 0:4], 1.0 / DH, None,
                                         ALU.mult, ALU.bypass), reads=[sums3], writes=[stats3])
                    p.op("dve", lambda e, ti=ti:
                         e.tensor_scalar(stats3[:, ti, :, 8:12], sumsAb[:, ti, :, 0:4], 1.0 / DH, None,
                                         ALU.mult, ALU.bypass), reads=[sumsAb], writes=[stats3])
                    p.op("dve", lambda e, ti=ti:
                         e.tensor_mul(tmp3b[:, ti, :, :], stats3[:, ti, :, 0:4], stats3[:, ti, :, 0:4]),
                         reads=[stats3], writes=[tmp3b])
                    p.op("dve", lambda e, ti=ti:
                         e.scalar_tensor_tensor(stats3[:, ti, :, 4:8], sums3[:, ti, :, 4:8], 1.0 / DH,
                                                tmp3b[:, ti, :, :], ALU.mult, ALU.subtract),
                         reads=[sums3, tmp3b], writes=[stats3])
                    p.op("act", lambda e, ti=ti:
                         e.activation(tmp3b[:, ti, :, :], sums3[:, ti, :, 4:8], AF.Ln),
                         reads=[sums3], writes=[tmp3b])
                    p.op("act", lambda e, ti=ti:
                         e.activation(stats3[:, ti, :, 12:16], tmp3b[:, ti, :, :], AF.Exp, scale=0.5),
                         reads=[tmp3b], writes=[stats3])
            def p3b_segA2(s, t0, ls_tok, ll_tok, dl_tok, vv_tok, x_tok3, xb_tok3, xb_f3,
                          sums3, stats3, statsT, sqscr3, tmp3b, sumsAb, rms_sc, hdn, lg3, coef, sc3,
                          o_tok, o_nb, o_f, xa, x2p):
                for ti in range(2):
                    for h in range(H):
                        p.op("pe", lambda e, stats3=stats3, ti=ti, h=h:
                             e.transpose(pST[0:16, h * 128:(h + 1) * 128], stats3[:, ti, h, :], identf[:]),
                             reads=[stats3, identf], writes=[pST], inc=(h == H - 1))
                    p.op("act", lambda e, stats3=stats3, statsT=statsT, ti=ti:
                         e.copy(statsT[0:16, :, ti * 128:(ti + 1) * 128],
                                pST[0:16, 0:512].rearrange("p (h c) -> p h c", h=H)),
                         reads=[pST], writes=[statsT])
            def p3b_segB1(s, t0, ls_tok, ll_tok, dl_tok, vv_tok, x_tok3, xb_tok3, xb_f3,
                          sums3, stats3, statsT, sqscr3, tmp3b, sumsAb, rms_sc, hdn, lg3, coef, sc3,
                          o_tok, o_nb, o_f, xa, x2p):
                for h in range(H):
                    for mt in range(8):
                        pz = (pL3, pG3, pZ3)[(h * 8 + mt) % 3]
                        p.op("pe", lambda e, pz=pz, statsT=statsT, h=h, mt=mt:
                             e.matmul(pz[:, 0:S3], w1s_l[0:16, mt * 128:(mt + 1) * 128], statsT[0:16, h, :],
                                      start=True, stop=False),
                             reads=[w1s_l, statsT], writes=[pz], inc=False)
                        for k in range(8):
                            p.op("pe", lambda e, pz=pz, xb_f3=xb_f3, mt=mt, k=k:
                                 e.matmul(pz[:, 0:S3], w1x_l[:, k, mt * 128:(mt + 1) * 128], xb_f3[:, k, :],
                                          start=False, stop=(k == 7)),
                                 reads=[w1x_l, xb_f3], writes=[pz], inc=(k == 7))
                        p.op("act", lambda e, pz=pz, hdn=hdn, h=h, mt=mt:
                             e.activation(hdn[:, mt, :], pz[:, 0:S3], AF.Gelu, bias=fb1v[:, mt:mt + 1]),
                             reads=[pz, fb1v], writes=[hdn])
                    for ti in range(2):
                        for k in range(8):
                            p.op("pe", lambda e, hdn=hdn, h=h, ti=ti, k=k:
                                 e.matmul(pTT[:, 256:260], hdn[:, k, ti * 128:(ti + 1) * 128], w2g_l[:, k, :],
                                          start=(k == 0), stop=(k == 7)),
                                 reads=[hdn, w2g_l], writes=[pTT], inc=(k == 7))
                        p.op("act", lambda e, lg3=lg3, h=h, ti=ti:
                             e.copy(lg3[:, ti, h, :], pTT[:, 256:260]),
                             reads=[pTT], writes=[lg3])

            def p3b_segB2(s, t0, ls_tok, ll_tok, dl_tok, vv_tok, x_tok3, xb_tok3, xb_f3,
                          sums3, stats3, statsT, sqscr3, tmp3b, sumsAb, rms_sc, hdn, lg3, coef, sc3,
                          o_tok, o_nb, o_f, xa, x2p):
                for ti in range(2):
                    p.op("dve", lambda e, lg3=lg3, ti=ti:
                         e.scalar_tensor_tensor(lg3[:, ti, :, :], lg3[:, ti, :, :], itemp_b[:, 0:1],
                                                b2g_it[:], ALU.mult, ALU.add),
                         reads=[lg3, itemp_b, b2g_it], writes=[lg3])
                p.op("dve", lambda e, lg3=lg3, sc3=sc3:
                     e.reduce_max(sc3[:, 0, :, 0:1].rearrange("p h f -> p (h f)")
                                  .rearrange("p (h f) -> p h f", f=1),
                                  lg3[:, 0, :, :], axis=AX.X, negate=True),
                     reads=[lg3], writes=[sc3])
                p.op("dve", lambda e, lg3=lg3, sc3=sc3:
                     e.reduce_max(sc3[:, 1, :, 0:1].rearrange("p h f -> p (h f)")
                                  .rearrange("p (h f) -> p h f", f=1),
                                  lg3[:, 1, :, :], axis=AX.X, negate=True),
                     reads=[lg3], writes=[sc3])
                for ti in range(2):
                    p.op("dve", lambda e, lg3=lg3, sc3=sc3, ti=ti:
                         e.tensor_add(lg3[:, ti, :, :], lg3[:, ti, :, :],
                                      sc3[:, ti, :, 0:1].broadcast_to([128, H, 4])),
                         reads=[lg3, sc3], writes=[lg3])
                p.op("act", lambda e, lg3=lg3:
                     e.activation(lg3[:].rearrange("p ti h f -> p (ti h f)"),
                                  lg3[:].rearrange("p ti h f -> p (ti h f)"), AF.Exp),
                     reads=[lg3], writes=[lg3])
                for ti in range(2):
                    p.op("dve", lambda e, lg3=lg3, sc3=sc3, ti=ti:
                         e.reduce_sum(sc3[:, ti, :, 1:2].rearrange("p h f -> p (h f)")
                                      .rearrange("p (h f) -> p h f", f=1),
                                      lg3[:, ti, :, :], axis=AX.X),
                         reads=[lg3], writes=[sc3])
                p.op("dve", lambda e, sc3=sc3:
                     e.reciprocal(sc3[:, :, :, 2:3].rearrange("p ti h f -> p (ti h f)"),
                                  sc3[:, :, :, 1:2].rearrange("p ti h f -> p (ti h f)")),
                     reads=[sc3], writes=[sc3])
                for ti in range(2):
                    p.op("dve", lambda e, lg3=lg3, coef=coef, sc3=sc3, ti=ti:
                         e.tensor_mul(coef[:, ti, :, :], lg3[:, ti, :, :],
                                      sc3[:, ti, :, 2:3].broadcast_to([128, H, 4])),
                         reads=[lg3, sc3], writes=[coef])
                for ti in range(2):
                    for h in range(H):
                        p.op("dve", lambda e, coef=coef, h=h, ti=ti:
                             e.tensor_add(coef[:, ti, h, 0:1], coef[:, ti, h, 0:1], rs_b[:]),
                             reads=[coef, rs_b], writes=[coef])
                        p.op("dve", lambda e, coef=coef, h=h, ti=ti:
                             e.tensor_add(coef[:, ti, h, 1:2], coef[:, ti, h, 1:2], rl_b[:]),
                             reads=[coef, rl_b], writes=[coef])
                oa = o_tok[0]
                for h in range(H):
                    p.op("dve", lambda e, oa=oa, h=h:
                         e.tensor_single_scalar(oa[:, h * DH:(h + 1) * DH], ls_tok[:, 0, h * DH:(h + 1) * DH],
                                                coef[:, 0, h, 0:1], ALU.mult),
                         reads=[ls_tok, coef], writes=[oa])
                    for ci, comp in ((1, ll_tok), (2, dl_tok), (3, vv_tok)):
                        p.op("dve", lambda e, oa=oa, h=h, ci=ci, comp=comp:
                             e.scalar_tensor_tensor(oa[:, h * DH:(h + 1) * DH],
                                                    comp[:, 0, h * DH:(h + 1) * DH],
                                                    coef[:, 0, h, ci:ci + 1],
                                                    oa[:, h * DH:(h + 1) * DH], ALU.mult, ALU.add),
                             reads=[comp, coef, oa], writes=[oa])
                oa1 = o_tok[1]
                mixtmp = sqBB[1]
                p.op("pool", lambda e, oa1=oa1:
                     e.tensor_mul(oa1[:].rearrange("p (h d) -> p h d", h=H),
                                  ls_tok[:, 1, :].rearrange("p (h d) -> p h d", h=H),
                                  coef[:, 1, :, 0:1].broadcast_to([128, H, DH])),
                     reads=[ls_tok, coef], writes=[oa1])
                for ci, comp in ((1, ll_tok), (2, dl_tok), (3, vv_tok)):
                    p.op("pool", lambda e, mixtmp=mixtmp, ci=ci, comp=comp:
                         e.tensor_mul(mixtmp[:, 0:D].rearrange("p (h d) -> p h d", h=H),
                                      comp[:, 1, :].rearrange("p (h d) -> p h d", h=H),
                                      coef[:, 1, :, ci:ci + 1].broadcast_to([128, H, DH])),
                         reads=[comp, coef], writes=[mixtmp])
                    p.op("pool", lambda e, oa1=oa1, mixtmp=mixtmp:
                         e.tensor_add(oa1[:], oa1[:], mixtmp[:, 0:D]),
                         reads=[mixtmp, oa1], writes=[oa1])
                for ti in range(2):
                    oa = o_tok[ti]
                    sqw = sqBB[ti]
                    p.op("act", lambda e, oa=oa, sqw=sqw, ti=ti:
                         e.activation(sqw[:, 0:D], oa[:], AF.Square),
                         reads=[oa], writes=[sqw])
                    p.op("dve", lambda e, sqw=sqw, ti=ti:
                         e.reduce_sum(rms_sc[:, ti, :, 0:1],
                                      sqw[:, 0:D].rearrange("p (h d) -> p h d", h=H), axis=AX.X),
                         reads=[sqw], writes=[rms_sc])
                    p.op("dve", lambda e, ti=ti:
                         e.tensor_scalar(rms_sc[:, ti, :, 1:2], rms_sc[:, ti, :, 0:1], 1.0 / DH, 1e-5,
                                         ALU.mult, ALU.add),
                         reads=[rms_sc], writes=[rms_sc])
                    p.op("act", lambda e, ti=ti:
                         e.activation(rms_sc[:, ti, :, 2:3], rms_sc[:, ti, :, 1:2], AF.Ln),
                         reads=[rms_sc], writes=[rms_sc])
                    p.op("act", lambda e, ti=ti:
                         e.activation(rms_sc[:, ti, :, 3:4], rms_sc[:, ti, :, 2:3], AF.Exp, scale=-0.5),
                         reads=[rms_sc], writes=[rms_sc])
                    p.op("dve", lambda e, oa=oa, ti=ti:
                         e.tensor_mul(o_nb[:, ti, :].rearrange("p (h d) -> p h d", h=H),
                                      oa[:].rearrange("p (h d) -> p h d", h=H),
                                      rms_sc[:, ti, :, 3:4].broadcast_to([128, H, DH])),
                         reads=[oa, rms_sc], writes=[o_nb])
            def p3b_segB2p(s, t0, ls_tok, ll_tok, dl_tok, vv_tok, x_tok3, xb_tok3, xb_f3,
                           sums3, stats3, statsT, sqscr3, tmp3b, sumsAb, rms_sc, hdn, lg3, coef, sc3,
                           o_tok, o_nb, o_f, xa, x2p):
                for ti in range(2):
                    for j in range(8):
                        p.op("pe", lambda e, o_nb=o_nb, ti=ti, j=j:
                             e.transpose(pTb2[:, j * 128:(j + 1) * 128], o_nb[:, ti, j * 128:(j + 1) * 128], identb[:]),
                             reads=[o_nb, identb], writes=[pTb2], inc=(j == 7))
                    p.op("act", lambda e, o_f=o_f, ti=ti:
                         e.copy(o_f[:, :, ti * 128:(ti + 1) * 128], pTb2[:, 0:1024].rearrange("p (j c) -> p j c", j=8)),
                         reads=[pTb2], writes=[o_f])
                for ti in range(2):
                    for hf in range(2):
                        for k in range(8):
                            p.op("pe", lambda e, ti=ti, hf=hf, k=k:
                                 e.matmul(pO3[:, 0:512], o_f[:, k, ti * 128:(ti + 1) * 128],
                                          wo_l[:, k, hf * 512:(hf + 1) * 512], start=(k == 0), stop=(k == 7)),
                                 reads=[o_f, wo_l], writes=[pO3], inc=(k == 7))
                        p.op("dve", lambda e, ti=ti, hf=hf:
                             e.tensor_add(xa[:, ti, hf * 512:(hf + 1) * 512], pO3[:, 0:512],
                                          x_tok3[:, ti, hf * 512:(hf + 1) * 512]),
                             reads=[pO3, x_tok3], writes=[xa])
                p.op("dve", lambda e:
                     e.reduce_sum(ln_sc[:, :, 0:1], xa[:].rearrange("p ti d -> p ti d"), axis=AX.X),
                     reads=[xa], writes=[ln_sc])
                p.op("dve", lambda e:
                     e.tensor_scalar(ln_sc[:, :, 1:2], ln_sc[:, :, 0:1], 1.0 / D, None, ALU.mult, ALU.bypass),
                     reads=[ln_sc], writes=[ln_sc])
                for ti in range(2):
                    sqw = sqBB[ti]
                    p.op("act", lambda e, sqw=sqw, ti=ti:
                         e.activation(sqw[:, 0:D], xa[:, ti, :], AF.Square),
                         reads=[xa], writes=[sqw])
                    p.op("dve", lambda e, sqw=sqw, ti=ti:
                         e.reduce_sum(ln_sc[:, ti:ti + 1, 2:3], sqw[:, 0:D].rearrange("p (o d) -> p o d", o=1),
                                      axis=AX.X),
                         reads=[sqw], writes=[ln_sc])
                p.op("dve", lambda e:
                     e.tensor_mul(ln_sc[:, :, 3:4], ln_sc[:, :, 1:2], ln_sc[:, :, 1:2]),
                     reads=[ln_sc], writes=[ln_sc])
                p.op("dve", lambda e:
                     e.scalar_tensor_tensor(ln_sc[:, :, 4:5], ln_sc[:, :, 2:3], 1.0 / D,
                                            ln_sc[:, :, 3:4], ALU.mult, ALU.subtract),
                     reads=[ln_sc], writes=[ln_sc])
                p.op("dve", lambda e:
                     e.tensor_scalar(ln_sc[:, :, 5:6], ln_sc[:, :, 4:5], 1.0, 1e-5, ALU.mult, ALU.add),
                     reads=[ln_sc], writes=[ln_sc])
                p.op("act", lambda e:
                     e.activation(ln_sc[:, :, 6:7], ln_sc[:, :, 5:6], AF.Ln),
                     reads=[ln_sc], writes=[ln_sc])
                p.op("act", lambda e:
                     e.activation(ln_sc[:, :, 7:8], ln_sc[:, :, 6:7], AF.Exp, scale=-0.5),
                     reads=[ln_sc], writes=[ln_sc])
                for ti in range(2):
                    p.op("dve", lambda e, ti=ti:
                         e.tensor_scalar(x2p[:, ti, :], xa[:, ti, :], ln_sc[:, ti, 1:2], ln_sc[:, ti, 7:8],
                                         ALU.subtract, ALU.mult),
                         reads=[xa, ln_sc], writes=[x2p])
                if "s_dbg1" in debug_taps:
                    p.dma("dact", s_dbg1[t0:t0 + S3, :].rearrange("(ti p) c -> p ti c", p=128),
                          stats3[:].rearrange("p ti h k -> p ti (h k)"))
                if "s_dbg2" in debug_taps:
                    p.dma("dact", s_dbg2[t0:t0 + S3, :].rearrange("(ti p) c -> p ti c", p=128),
                          coef[:].rearrange("p ti h k -> p ti (h k)"))
                if "s_dbg4" in debug_taps:
                    p.dma("dpool", s_dbg4[t0:t0 + S3, :].rearrange("(ti p) d -> p ti d", p=128), o_nb[:])
                if "s_dbg5" in debug_taps and s == NSEG3 - 1:
                    p.dma("dpool", s_dbg5.rearrange("(k p) m -> p k m", p=128), wo_l[:])
                    for kk2 in range(8):
                        p.dma("dpool", s_dbg6[0:S3, kk2 * 128:(kk2 + 1) * 128].rearrange("(ti p) c -> p ti c", p=128),
                              o_f[:, kk2, :].rearrange("p (ti tt) -> p ti tt", ti=2))
                if "s_dbg3" in debug_taps:
                    p.dma("dact", s_dbg3[t0:t0 + S3, :].rearrange("(ti p) d -> p ti d", p=128), xa[:])
                p.dma("dact", s_x2[t0:t0 + S3, :].rearrange("(ti p) d -> p ti d", p=128), x2p[:])

            def p3b_args(s):
                pcs = s % 2
                return (s, s * S3, ls_tokB[pcs], ll_tokB[pcs], dl_tokB[pcs], vv_tokB[pcs],
                        x_tok3B[s % 3], xb_tok3B[pcs], xb_f3B[pcs], sums3B[pcs], stats3B[pcs],
                        statsTB[pcs], sqscr3B[pcs], tmp3bB[pcs], sumsAbB[pcs], rms_scB[pcs], hdnB[pcs], lg3B[pcs],
                        coefB[pcs], sc3B[pcs], o_tokB[pcs], o_nbB[pcs], o_fB[pcs], xaB[pcs], x2pB[pcs])

            if p3b_segs > 0:
                p3b_segA(*p3b_args(0))
                if p3b_segs > 1:
                    p3b_segA(*p3b_args(1))
                p3b_segA2(*p3b_args(0))
                p3b_segB1(*p3b_args(0))
            for s in range(p3b_segs):
                if s + 1 < p3b_segs:
                    p3b_segA2(*p3b_args(s + 1))
                p3b_segB2(*p3b_args(s))
                if s + 1 < p3b_segs:
                    p3b_segB1(*p3b_args(s + 1))
                if s + 2 < p3b_segs:
                    p3b_segA(*p3b_args(s + 2))
                p3b_segB2p(*p3b_args(s))

        p.barrier()

        if max_phase < 4.5:
            p.barrier_dma_done()
            p.emit()
            return nc
        with contextlib.ExitStack() as s4:
            sb4 = lambda name, shape, dt: s4.enter_context(nc.sbuf_tensor(name, shape, dt))
            ps4_ = lambda name, shape, dt: s4.enter_context(nc.psum_tensor(name, shape, dt))

            fw1_l = sb4("fw1_l", [128, 8, FF], BF16)
            fw2_l = sb4("fw2_l", [128, 32, D], BF16)
            fb1s = sb4("fb1s", [128, 32], F32)
            gammab = sb4("gammab", [128, D], F32)
            betab = sb4("betab", [128, D], F32)
            rowbuf = sb4("rowbuf", [1, D], F32)

            x2_tok = [sb4(f"x2_tok{i}", [128, 2, D], F32) for i in range(3)]
            x2_f = [sb4(f"x2_f{i}", [128, 8, S4], BF16) for i in range(2)]
            h_fq = [sb4(f"h_fq{i}", [128, 8, S4], BF16) for i in range(4)]
            y1_s = sb4("y1_s", [128, 2, D], F32)
            y1 = [y1_s, y1_s]
            out_t = [y1_s, y1_s]
            sq4 = sb4("sq4", [128, D], BF16)
            sc4 = [sb4(f"sc4_{i}", [128, 2, 8], F32) for i in range(2)]
            fb2row = sb4("fb2row", [1, D], F32)

            pT4 = ps4_("pT4", [128, 512], F32)
            pT4b = ps4_("pT4b", [128, 512], F32)
            pH4 = ps4_("pH4", [128, 512], F32)
            pH4b = ps4_("pH4b", [128, 512], F32)
            pO4 = ps4_("pO4", [128, 512], F32)
            pO4b = ps4_("pO4b", [128, 512], F32)
            pB4 = ps4_("pB4", [128, 512], F32)

            p.dma("dpool", fw1_l[:], fw1_h.rearrange("(k p) m -> p k m", p=128))
            p.dma("dpool", fw2_l[:], fw2_h.rearrange("(k p) m -> p k m", p=128))
            p.dma("dsp", fb1s[:], fb1f_h)
            p.dma("dsp", fb2row[:], fb2f_h)
            for (row_h, dstb) in ((gamma_h, gammab), (beta_h, betab)):
                p.dma("dsp", rowbuf[:], row_h)
                for hf in range(2):
                    p.op("pe", lambda e, hf=hf: e.matmul(pB4[:, 0:512], ones1f[:], rowbuf[0:1, hf * 512:(hf + 1) * 512],
                                                         start=True, stop=True),
                         reads=[ones1f, rowbuf], writes=[pB4])
                    p.op("act", lambda e, dstb=dstb, hf=hf: e.copy(dstb[:, hf * 512:(hf + 1) * 512], pB4[:, 0:512]),
                         reads=[pB4], writes=[dstb])

            def ln_stats(src_ap, scb, msl, vsl, rsl, reads):
                p.op("dve", lambda e: e.reduce_sum(msl, src_ap, axis=AX.X), reads=reads, writes=[scb])
                p.op("dve", lambda e: e.tensor_scalar(msl, msl, 1.0 / D, None, ALU.mult, ALU.bypass),
                     reads=[scb], writes=[scb])
                p.op("dve", lambda e: e.memset(vsl, 0.0), writes=[scb])
                p.op("act", lambda e: e.activation(sq4[:], src_ap, AF.Square, accum_out=vsl),
                     reads=reads, writes=[sq4, scb])
                p.op("dve", lambda e: e.tensor_mul(rsl, msl, msl), reads=[scb], writes=[scb])
                p.op("dve", lambda e: e.scalar_tensor_tensor(vsl, vsl, 1.0 / D, rsl, ALU.mult, ALU.subtract),
                     reads=[scb], writes=[scb])
                p.op("dve", lambda e: e.tensor_scalar(vsl, vsl, 1.0, 1e-5, ALU.mult, ALU.add),
                     reads=[scb], writes=[scb])
                p.op("act", lambda e: e.activation(rsl, vsl, AF.Ln), reads=[scb], writes=[scb])
                p.op("act", lambda e: e.activation(rsl, rsl, AF.Exp, scale=-0.5), reads=[scb], writes=[scb])

            for s in range(NSEG4):
                t0 = s * S4
                pc = s % 2
                x2t, x2f, yy, scb = x2_tok[s % 3], x2_f[pc], y1[pc], sc4[pc]
                p.dma("dsp", x2t[:], s_x2[t0:t0 + S4, :].rearrange("(ti p) d -> p ti d", p=128))
                for ti in range(2):
                    for jh in range(2):
                        pz4 = pT4 if (ti * 2 + jh) % 2 == 0 else pT4b
                        for j in range(4):
                            p.op("pe", lambda e, pz4=pz4, x2t=x2t, ti=ti, jh=jh, j=j:
                                 e.transpose(pz4[:, j * 128:(j + 1) * 128],
                                             x2t[:, ti, (jh * 4 + j) * 128:(jh * 4 + j + 1) * 128], identf[:]),
                                 reads=[x2t, identf], writes=[pz4], inc=(j == 3))
                        p.op("act", lambda e, pz4=pz4, x2f=x2f, ti=ti, jh=jh:
                             e.copy(x2f[:, jh * 4:(jh + 1) * 4, ti * 128:(ti + 1) * 128],
                                    pz4[:, 0:512].rearrange("p (j c) -> p j c", j=4)),
                             reads=[pz4], writes=[x2f])
                for mt in range(32):
                    pp = pH4 if mt % 2 == 0 else pH4b
                    hq = h_fq[mt // 8]
                    for k in range(8):
                        p.op("pe", lambda e, pp=pp, x2f=x2f, mt=mt, k=k:
                             e.matmul(pp[:, 0:S4], fw1_l[:, k, mt * 128:(mt + 1) * 128], x2f[:, k, :],
                                      start=(k == 0), stop=(k == 7)),
                             reads=[fw1_l, x2f], writes=[pp], inc=(k == 7))
                    p.op("act", lambda e, pp=pp, hq=hq, mt=mt:
                         e.activation(hq[:, mt % 8, :], pp[:, 0:S4], AF.Relu, bias=fb1s[:, mt:mt + 1]),
                         reads=[pp, fb1s], writes=[hq])
                for ti in range(2):
                    for hf in range(2):
                        pq = pO4 if (ti * 2 + hf) % 2 == 0 else pO4b
                        for k in range(32):
                            hq = h_fq[k // 8]
                            p.op("pe", lambda e, pq=pq, hq=hq, ti=ti, hf=hf, k=k:
                                 e.matmul(pq[:, 0:512], hq[:, k % 8, ti * 128:(ti + 1) * 128],
                                          fw2_l[:, k, hf * 512:(hf + 1) * 512], start=(k == 0), stop=False),
                                 reads=[hq, fw2_l], writes=[pq], inc=False)
                        p.op("pe", lambda e, pq=pq, hf=hf:
                             e.matmul(pq[:, 0:512], ones1f[:], fb2row[0:1, hf * 512:(hf + 1) * 512],
                                      start=False, stop=True),
                             reads=[ones1f, fb2row], writes=[pq], inc=True)
                        p.op("dve", lambda e, pq=pq, yy=yy, x2t=x2t, ti=ti, hf=hf:
                             e.tensor_add(yy[:, ti, hf * 512:(hf + 1) * 512], pq[:, 0:512],
                                          x2t[:, ti, hf * 512:(hf + 1) * 512]),
                             reads=[pq, x2t], writes=[yy])
                    ln_stats(yy[:, ti, :], scb, scb[:, ti, 0:1], scb[:, ti, 1:2], scb[:, ti, 2:3], [yy])
                    p.op("dve", lambda e, yy=yy, scb=scb, ti=ti:
                         e.tensor_scalar(yy[:, ti, :], yy[:, ti, :], scb[:, ti, 0:1], scb[:, ti, 2:3],
                                         ALU.subtract, ALU.mult),
                         reads=[yy, scb], writes=[yy])
                    p.op("dve", lambda e, yy=yy, ti=ti:
                         e.tensor_mul(yy[:, ti, :], yy[:, ti, :], gammab[:]),
                         reads=[yy, gammab], writes=[yy])
                    p.op("dve", lambda e, yy=yy, ti=ti:
                         e.tensor_add(yy[:, ti, :], yy[:, ti, :], betab[:]),
                         reads=[yy, betab], writes=[yy])
                    p.op("dve", lambda e, yy=yy, x2t=x2t, ti=ti:
                         e.tensor_add(yy[:, ti, :], yy[:, ti, :], x2t[:, ti, :]),
                         reads=[yy, x2t], writes=[yy])
                    ln_stats(yy[:, ti, :], scb, scb[:, ti, 3:4], scb[:, ti, 4:5], scb[:, ti, 5:6], [yy])
                    p.op("dve", lambda e, yy=yy, scb=scb, ti=ti:
                         e.tensor_scalar(yy[:, ti, :], yy[:, ti, :], scb[:, ti, 3:4], scb[:, ti, 5:6],
                                         ALU.subtract, ALU.mult),
                         reads=[yy, scb], writes=[yy])
                p.dma("dact", out_h[t0:t0 + S4, :].rearrange("(ti p) d -> p ti d", p=128), yy[:])

        p.barrier()
        p.barrier_dma_done()
        p.emit()
    return nc



def _layout_inputs(inputs):
    g = lambda k: np.ascontiguousarray(inputs[k])
    fw1 = inputs["L0_ffn_w1"][0]
    fw2 = inputs["L0_ffn_w2"][0]
    fb1 = inputs["L0_ffn_b1"][0]
    fb2 = inputs["L0_ffn_b2"][0]
    fuse_w1 = inputs["L0_attn_fuse_w1"]
    common = {
        "wq_t": np.ascontiguousarray(inputs["L0_attn_q_proj_w"].T),
        "wk_t": np.ascontiguousarray(inputs["L0_attn_k_proj_w"].T),
        "wv_t": np.ascontiguousarray(inputs["L0_attn_v_proj_w"].T),
        "wb_t": np.ascontiguousarray(inputs["L0_attn_b_proj_w"].T),
        "qconv": g("L0_attn_q_conv_w"),
        "kconv": g("L0_attn_k_conv_w"),
        "vconv": g("L0_attn_v_conv_w"),
        "firS": np.ascontiguousarray(inputs["L0_attn_fir_short_w"].reshape(D, 5)),
        "firL": np.ascontiguousarray(inputs["L0_attn_fir_long_w"].reshape(D, 64)),
        "w1x_t": np.ascontiguousarray(fuse_w1[:, :D].T),
        "w1s_t": np.ascontiguousarray(fuse_w1[:, D:D + 16].T[[(r % 4) * 4 + r // 4 for r in range(16)]]),
        "fuse_b1v": np.ascontiguousarray(inputs["L0_attn_fuse_b1"].reshape(8, 128).T),
        "w2g_t": np.ascontiguousarray(inputs["L0_attn_fuse_w2"].T),
        "fuse_b2v": np.ascontiguousarray(inputs["L0_attn_fuse_b2"].reshape(1, H)),
        "ltemp": np.ascontiguousarray(inputs["L0_attn_logit_temp"].reshape(1, 1)),
        "rs": np.ascontiguousarray(inputs["L0_attn_res_scale_short"].reshape(1, 1)),
        "rl": np.ascontiguousarray(inputs["L0_attn_res_scale_long"].reshape(1, 1)),
        "onorm": np.ascontiguousarray(inputs["L0_attn_o_norm_w"].reshape(2, 128).T),
        "wo_t": np.ascontiguousarray(inputs["L0_attn_o_proj_w"].T),
        "fw1": np.ascontiguousarray(fw1),
        "fb1": np.ascontiguousarray(fb1.reshape(32, 128).T),
        "fw2": np.ascontiguousarray(fw2),
        "fb2": np.ascontiguousarray(fb2.reshape(1, D)),
        "gamma": np.ascontiguousarray(inputs["L0_ffn_gamma"].reshape(1, D)),
        "beta": np.ascontiguousarray(inputs["L0_ffn_beta"].reshape(1, D)),
    }
    common = {k: v.astype(np.float32, copy=False) for k, v in common.items()}
    x = inputs["L0_attn_hidden_states"]
    in_maps = []
    for b in range(8):
        m = dict(common)
        m["x"] = np.ascontiguousarray(x[b]).astype(np.float32, copy=False)
        in_maps.append(m)
    return in_maps


def kernel(**inputs):
    in_maps = _layout_inputs(inputs)
    nc = build_program()
    res = run_bass_kernel_spmd(nc, in_maps, core_ids=list(range(8)))
    out = np.stack([r["out"] for r in res.results], axis=0)
    return out.astype(np.float32)


# version 28 (final)
# speedup vs baseline: 1.2671x; 1.0460x over previous
import contextlib
import math

import numpy as np

import concourse.bass as bass
import concourse.mybir as mybir
from concourse.bass_utils import run_bass_kernel_spmd

F32 = mybir.dt.float32
BF16 = mybir.dt.bfloat16
AF = mybir.ActivationFunctionType
ALU = mybir.AluOpType
AX = mybir.AxisListType

T = 4096
D = 1024
H = 4
DH = 256
CH = 128
NCH = T // CH
S1 = 512
NSEG1 = T // S1
S3 = 256
NSEG3 = T // S3
S4 = 256
NSEG4 = T // S4
FF = 4096

ENGS = ("pe", "act", "dve", "pool", "sp")
DMAQ = {"dsp": "sp", "dact": "act", "dpool": "pool"}
ALLC = list(ENGS) + list(DMAQ)


def _nm(t):
    if hasattr(t, "tensor"):
        t = t.tensor
    return t.name


class Prog:

    R = 8

    def __init__(self, nc):
        self.nc = nc
        self.ops = {e: [] for e in ENGS}
        self.cnt = {e: 0 for e in ALLC}
        self.vc = {e: {} for e in ENGS}
        self.hist = {e: [] for e in ALLC}
        self.writer = {}
        self.readers = {}
        self.sems = {}
        self.n_ops = 0

    def _q_waits(self, q, k):
        R = self.R
        out = []
        for i in range(min(k, R)):
            cnt_i = (k - 1 - i) // R + 1
            out.append((f"{q}{i}", 16 * cnt_i))
        return out

    def _need_for(self, reads, writes):
        need = {}

        def add(f, c):
            need[f] = max(need.get(f, 0), c)

        for t in reads:
            w = self.writer.get(_nm(t))
            if w:
                add(*w)
        for t in writes:
            w = self.writer.get(_nm(t))
            if w:
                add(*w)
            for f, c in self.readers.get(_nm(t), {}).items():
                add(f, c)
        return need

    def _waits(self, e, need):
        waits = []
        myvc = self.vc[e]
        for f in sorted(need):
            c = need[f]
            if c > self.cnt[f] + (1 if f == e else 0):
                raise RuntimeError(f"op on {e} needs {f}@{c} but {f} only at {self.cnt[f]} (missing milestone inc)")
            if f == e and (c > self.cnt[f] or e == "pe"):
                continue
            if myvc.get(f, 0) < c:
                if f in DMAQ:
                    k = c // 16
                    for (sname, sval) in self._q_waits(f, k):
                        if myvc.get(sname, 0) < sval:
                            waits.append((sname, sval))
                            myvc[sname] = sval
                    for idx in range(myvc.get(f, 0) // 16, k):
                        h = self.hist[f][idx]
                        for kk, v in h.items():
                            if myvc.get(kk, 0) < v:
                                myvc[kk] = v
                else:
                    waits.append((f, c))
                    h = self.hist[f][c - 1]
                    for kk, v in h.items():
                        if myvc.get(kk, 0) < v:
                            myvc[kk] = v
                if myvc.get(f, 0) < c:
                    myvc[f] = c
        return waits

    def _record(self, e, c_eff, reads, writes):
        for t in writes:
            self.writer[_nm(t)] = (e, c_eff)
            self.readers.pop(_nm(t), None)
        for t in reads:
            self.readers.setdefault(_nm(t), {})[e] = c_eff

    @staticmethod
    def _is_psum(t):
        h = t.tensor if hasattr(t, "tensor") else t
        return "PSum" in type(h).__name__ or "psum" in str(getattr(h, "space", "")).lower()

    def op(self, e, fn, reads=(), writes=(), inc=True):
        reads = list(reads)
        writes = list(writes) + [t for t in reads if self._is_psum(t)]
        reads = [t for t in reads if not self._is_psum(t)]
        need = self._need_for(reads, writes)
        waits = self._waits(e, need)
        incs = []
        if inc:
            self.cnt[e] += 1
            h = dict(self.vc[e])
            h[e] = self.cnt[e]
            self.hist[e].append(h)
            incs = [(e, 1)]
            c_eff = self.cnt[e]
        else:
            c_eff = self.cnt[e] + 1
        self.ops[e].append((waits, fn, incs))
        self._record(e, c_eff, reads, writes)
        self.n_ops += 1

    def dma(self, q, out_ap, in_ap, reads=(), writes=(), **kw):
        e = DMAQ[q]
        reads = list(reads) + [in_ap]
        writes = list(writes) + [out_ap]
        need = self._need_for(reads, writes)
        waits = self._waits(e, need)
        k = self.cnt[q] // 16 + 1
        sub = (k - 1) % self.R
        sname = f"{q}{sub}"
        if k > self.R:
            prev_target = 16 * ((k - 1) // self.R)
            myvc = self.vc[e]
            if myvc.get(sname, 0) < prev_target:
                waits.append((sname, prev_target))
                myvc[sname] = prev_target
                h = self.hist[q][k - self.R - 1]
                for kk, v in h.items():
                    if myvc.get(kk, 0) < v:
                        myvc[kk] = v
        self.cnt[q] += 16
        qvc = dict(self.vc[e])
        qvc[sname] = 16 * ((k - 1) // self.R + 1)
        self.hist[q].append(qvc)
        c = self.cnt[q]
        self.ops[e].append((waits, lambda eng, o=out_ap, i=in_ap, kw2=dict(kw): eng.dma_start(out=o, in_=i, **kw2), [(sname, 16)]))
        self._record(q, c, reads, writes)
        self.n_ops += 1

    def barrier(self):
        snapshot = {f: self.cnt[f] for f in ALLC}
        for e in ENGS:
            waits = []
            for f, c in snapshot.items():
                if f == e or c == 0:
                    continue
                if f in DMAQ:
                    waits.extend(self._q_waits(f, c // 16))
                else:
                    waits.append((f, c))
                self.vc[e][f] = max(self.vc[e].get(f, 0), c)
            for f, c in snapshot.items():
                if f in DMAQ:
                    for idx in range(c // 16):
                        for kk, v in self.hist[f][idx].items():
                            if self.vc[e].get(kk, 0) < v:
                                self.vc[e][kk] = v
                elif c > 0 and f != e:
                    for kk, v in self.hist[f][c - 1].items():
                        if self.vc[e].get(kk, 0) < v:
                            self.vc[e][kk] = v
            self.cnt[e] += 1
            h = dict(self.vc[e])
            h[e] = self.cnt[e]
            self.hist[e].append(h)
            self.ops[e].append((waits, self._flush_fn(e), [(e, 1)]))

    def _flush_fn(self, e):
        if e in ("pe", "sp"):
            return lambda eng: eng.nop(nofuse=True)
        dummy = self.dummy[e]
        if e == "act":
            return lambda eng: eng.memzero(dummy[:])
        return lambda eng: eng.memset(dummy[:], 0.0)

    def barrier_dma_done(self):
        need = {q: self.cnt[q] for q in DMAQ if self.cnt[q] > 0}
        waits = self._waits("sp", need)
        self.ops["sp"].append((waits, lambda eng: eng.nop(nofuse=True), []))

    def emit(self):
        nc = self.nc
        with nc.Block() as block:
            def mk(ename):
                def body(eng):
                    for waits, fn, incs in self.ops[ename]:
                        for (f, v) in waits:
                            eng.wait_ge(self.sems[f], v)
                        inst = fn(eng)
                        for (f, a) in incs:
                            inst.then_inc(self.sems[f], a)
                return body
            block.tensor(mk("pe"))
            block.scalar(mk("act"))
            block.vector(mk("dve"))
            block.gpsimd(mk("pool"))
            block.sync(mk("sp"))



def build_program(debug_taps=(), max_phase=9, p2_chunks=NCH, p2_stage=9, p3a_segs=NSEG3, p3b_segs=NSEG3, p3b_weights=True, p3a_stage=9, skip_fold=False):
    nc = bass.Bass("TRN2", target_bir_lowering=False)

    def din(name, shape, dt=F32):
        return nc.dram_tensor(name, shape, dt, kind="ExternalInput").ap()

    x_h = din("x", [T, D])
    wq_h = din("wq_t", [D, D])
    wk_h = din("wk_t", [D, D])
    wv_h = din("wv_t", [D, D])
    wb_h = din("wb_t", [D, H])
    qc_h = din("qconv", [D, 4])
    kc_h = din("kconv", [D, 4])
    vc_h = din("vconv", [D, 4])
    firS_h = din("firS", [D, 5])
    firL_h = din("firL", [D, 64])
    w1x_h = din("w1x_t", [D, D])
    w1s_h = din("w1s_t", [16, D])
    fb1_h = din("fuse_b1v", [128, 8])
    w2g_h = din("w2g_t", [D, H])
    fb2_h = din("fuse_b2v", [1, H])
    ltemp_h = din("ltemp", [1, 1])
    rs_h = din("rs", [1, 1])
    rl_h = din("rl", [1, 1])
    onorm_h = din("onorm", [128, 2])
    wo_h = din("wo_t", [D, D])
    fw1_h = din("fw1", [D, FF])
    fb1f_h = din("fb1", [128, 32])
    fw2_h = din("fw2", [FF, D])
    fb2f_h = din("fb2", [1, D])
    gamma_h = din("gamma", [1, D])
    beta_h = din("beta", [1, D])

    out_h = nc.dram_tensor("out", [T, D], F32, kind="ExternalOutput").ap()

    def scratch(name, shape, dt):
        kind = "ExternalOutput" if name in debug_taps else "Internal"
        return nc.dram_tensor(name, shape, dt, kind=kind).ap()

    s_qf = scratch("s_qf", [8, 128, T], BF16)
    s_kf = scratch("s_kf", [8, 128, T], BF16)
    s_vf = scratch("s_vf", [8, 128, 128 + T], BF16)
    s_vtok = scratch("s_vtok", [T, D], F32)
    s_blog = scratch("s_blog", [H, T], F32)
    s_delta = scratch("s_delta", [T, D], F32)
    s_x2 = scratch("s_x2", [T, D], F32)
    s_ls = scratch("s_ls", [T, D], F32)
    s_ll = scratch("s_ll", [T, D], F32)
    s_dbg1 = scratch("s_dbg1", [T, 64], F32)
    s_dbg2 = scratch("s_dbg2", [T, 16], F32)
    s_dbg3 = scratch("s_dbg3", [T, D], F32)
    s_dbg4 = scratch("s_dbg4", [T, D], F32)
    s_dbg5 = scratch("s_dbg5", [D, D], F32)
    s_dbg7 = scratch("s_dbg7", [D, D], F32)
    s_dbg6 = scratch("s_dbg6", [T, D], F32)

    p = Prog(nc)
    st = contextlib.ExitStack()
    with st:
        for e in ENGS:
            p.sems[e] = st.enter_context(nc.semaphore(f"sem_{e}"))
        p.dummy = {e: st.enter_context(nc.sbuf_tensor(f"dummy_{e}", [1, 8], F32))
                   for e in ("act", "dve", "pool")}
        for q in DMAQ:
            for i in range(Prog.R):
                p.sems[f"{q}{i}"] = st.enter_context(nc.semaphore(f"sem_{q}{i}"))

        sb = lambda name, shape, dt: st.enter_context(nc.sbuf_tensor(name, shape, dt))
        ps = lambda name, shape, dt: st.enter_context(nc.psum_tensor(name, shape, dt))

        identb = sb("identb", [128, 128], BF16)
        identf = sb("identf", [128, 128], F32)
        did2 = sb("did2", [128, 64], BF16)
        mask_lt = sb("mask_lt", [128, 128], F32)
        mask_ut = sb("mask_ut", [128, 128], F32)
        ones1f = sb("ones1f", [1, 128], F32)
        tiny = sb("tiny", [1, 8], F32)
        itemp_b = sb("itemp_b", [128, 1], F32)
        rs_b = sb("rs_b", [128, 1], F32)
        rl_b = sb("rl_b", [128, 1], F32)
        b2g_b = sb("b2g_b", [128, H], F32)
        onorm_s = sb("onorm_s", [128, 2], F32)

        p.op("pool", lambda e: e.memset(identb[:], 1.0), writes=[identb])
        p.op("pool", lambda e: e.affine_select(out=identb[:], in_=identb[:], compare_op=ALU.is_equal,
                                               fill=0.0, base=0, pattern=[[-1, 128]], channel_multiplier=1),
             reads=[identb], writes=[identb])
        p.op("pool", lambda e: e.memset(identf[:], 1.0), writes=[identf])
        p.op("pool", lambda e: e.affine_select(out=identf[:], in_=identf[:], compare_op=ALU.is_equal,
                                               fill=0.0, base=0, pattern=[[-1, 128]], channel_multiplier=1),
             reads=[identf], writes=[identf])
        p.op("pool", lambda e: e.memset(mask_lt[:], 1.0), writes=[mask_lt])
        p.op("pool", lambda e: e.affine_select(out=mask_lt[:], in_=mask_lt[:], compare_op=ALU.is_gt,
                                               fill=0.0, base=0, pattern=[[-1, 128]], channel_multiplier=1),
             reads=[mask_lt], writes=[mask_lt])
        p.op("pool", lambda e: e.memset(mask_ut[:], 1.0), writes=[mask_ut])
        p.op("pool", lambda e: e.affine_select(out=mask_ut[:], in_=mask_ut[:], compare_op=ALU.is_gt,
                                               fill=0.0, base=1, pattern=[[1, 128]], channel_multiplier=-1),
             reads=[mask_ut], writes=[mask_ut])
        p.op("pool", lambda e: e.memset(ones1f[:], 1.0), writes=[ones1f])
        p.dma("dpool", did2[0:64, :], identb[0:64, 0:64])
        p.dma("dpool", did2[64:128, :], identb[0:64, 0:64])

        p.dma("dsp", tiny[:, 0:1], ltemp_h)
        p.op("act", lambda e: e.activation(tiny[:, 1:2], tiny[:, 0:1], AF.Abs), reads=[tiny], writes=[tiny])
        p.op("act", lambda e: e.activation(tiny[:, 2:3], tiny[:, 1:2], AF.Exp, scale=-1.0), reads=[tiny], writes=[tiny])
        p.op("act", lambda e: e.activation(tiny[:, 3:4], tiny[:, 2:3], AF.Ln, bias=1.0), reads=[tiny], writes=[tiny])
        p.op("act", lambda e: e.activation(tiny[:, 4:5], tiny[:, 0:1], AF.Relu), reads=[tiny], writes=[tiny])
        p.op("dve", lambda e: e.tensor_add(tiny[:, 5:6], tiny[:, 3:4], tiny[:, 4:5]), reads=[tiny], writes=[tiny])
        p.op("dve", lambda e: e.tensor_scalar(tiny[:, 6:7], tiny[:, 5:6], 1e-4, None, ALU.add, ALU.bypass),
             reads=[tiny], writes=[tiny])
        p.op("dve", lambda e: e.reciprocal(tiny[:, 7:8], tiny[:, 6:7]), reads=[tiny], writes=[tiny])

        row4 = sb("row4", [1, H], F32)
        with contextlib.ExitStack() as s0:
            ps_bc = s0.enter_context(nc.psum_tensor("ps_bc", [128, 512], F32))

            def bcast_row(src_ap, dst_ap, n, reads):
                p.op("pe", lambda e: e.matmul(ps_bc[:, 0:n], ones1f[:], src_ap, start=True, stop=True),
                     reads=[ones1f] + reads, writes=[ps_bc])
                p.op("act", lambda e: e.copy(dst_ap, ps_bc[:, 0:n]), reads=[ps_bc], writes=[dst_ap])

            bcast_row(tiny[:, 7:8], itemp_b[:], 1, [tiny])
            p.dma("dsp", tiny[:, 0:1], rs_h)
            bcast_row(tiny[:, 0:1], rs_b[:], 1, [tiny])
            p.dma("dsp", tiny[:, 1:2], rl_h)
            bcast_row(tiny[:, 1:2], rl_b[:], 1, [tiny])
            p.dma("dsp", row4[:], fb2_h)
            bcast_row(row4[:], b2g_b[:], H, [row4])
            p.dma("dsp", onorm_s[:], onorm_h)
        p.barrier()

        with contextlib.ExitStack() as s1:
            sb1 = lambda name, shape, dt: s1.enter_context(nc.sbuf_tensor(name, shape, dt))
            ps1 = lambda name, shape, dt: s1.enter_context(nc.psum_tensor(name, shape, dt))

            wq_l = sb1("wq_l", [128, 8, D], BF16)
            wk_l = sb1("wk_l", [128, 8, D], BF16)
            wv_l = sb1("wv_l", [128, 8, D], BF16)
            wb_l = sb1("wb_l", [128, 8, H], BF16)
            qcw = sb1("qcw", [128, 8, 4], F32)
            kcw = sb1("kcw", [128, 8, 4], F32)
            vcw = sb1("vcw", [128, 8, 4], F32)
            qcd = sb1("qcd", [128, 8 * 4 * 128], BF16)
            kcd = sb1("kcd", [128, 8 * 4 * 128], BF16)
            vcd = sb1("vcd", [128, 8 * 4 * 128], BF16)

            for w_h, w_l in ((wq_h, wq_l), (wk_h, wk_l), (wv_h, wv_l)):
                p.dma("dpool", w_l[:], w_h.rearrange("(k p) m -> p k m", p=128))
            p.dma("dpool", wb_l[:], wb_h.rearrange("(k p) m -> p k m", p=128))
            for c_h, c_w in ((qc_h, qcw), (kc_h, kcw), (vc_h, vcw)):
                p.dma("dsp", c_w[:], c_h.rearrange("(c p) k -> p c k", p=128))
            for cw, cd in ((qcw, qcd), (kcw, kcd), (vcw, vcd)):
                for m in range(8):
                    for k in range(4):
                        o = (m * 4 + k) * 128
                        p.op("dve", lambda e, cw=cw, cd=cd, m=m, k=k, o=o:
                             e.tensor_scalar(cd[:, o:o + 128], identb[:], cw[:, m, 3 - k:4 - k], None, ALU.mult, ALU.bypass),
                             reads=[identb, cw], writes=[cd])

            if "s_dbg7" in debug_taps:
                p.dma("dpool", s_dbg7.rearrange("(k p) m -> p k m", p=128), wq_l[:])
            z64 = sb1("z64", [128, 128], BF16)
            p.op("pool", lambda e: e.memset(z64[:], 0.0), writes=[z64])
            for m in range(8):
                p.dma("dsp", s_vf[m, :, 0:128], z64[:])

            xb_tok = sb1("xb_tok", [128, 4, D], BF16)
            xb_f = sb1("xb_f", [128, 8, S1], BF16)
            q_s = sb1("q_s", [128, 8, S1 + 3], BF16)
            k_s = sb1("k_s", [128, 8, S1 + 3], BF16)
            v_s = sb1("v_s", [128, 8, S1 + 3], BF16)
            q_park = sb1("q_park", [128, 8, S1], BF16)
            k_park = sb1("k_park", [128, 8, S1], BF16)
            v_park = sb1("v_park", [128, 8, S1], BF16)
            v_f32 = sb1("v_f32", [128, S1], F32)
            v_tok = sb1("v_tok", [128, 4, D], F32)
            blog_s = sb1("blog_s", [H, S1], F32)

            pA = ps1("pA", [128, 1024], BF16)
            pB = ps1("pB", [128, 512], F32)
            pB2 = ps1("pB2", [128, 512], F32)
            pC = ps1("pC", [128, 512], F32)
            pC2 = ps1("pC2", [128, 512], F32)
            pD = ps1("pD", [4, 512], F32)
            pE = ps1("pE", [128, 512], F32)

            for m in range(8):
                p.op("pool", lambda e, m=m: e.memset(q_s[:, m, 0:3], 0.0), writes=[q_s])
                p.op("pool", lambda e, m=m: e.memset(k_s[:, m, 0:3], 0.0), writes=[k_s])
                p.op("pool", lambda e, m=m: e.memset(v_s[:, m, 0:3], 0.0), writes=[v_s])

            for s in range(NSEG1):
                t0 = s * S1
                p.dma("dpool", xb_tok[:], x_h[t0:t0 + S1, :].rearrange("(ti p) d -> p ti d", p=128))
                for ti in range(4):
                    for j in range(8):
                        p.op("pe", lambda e, ti=ti, j=j:
                             e.transpose(pA[:, j * 128:(j + 1) * 128], xb_tok[:, ti, j * 128:(j + 1) * 128], identb[:]),
                             reads=[xb_tok, identb], writes=[pA], inc=(j == 7))
                    for j in range(8):
                        p.op("act", lambda e, ti=ti, j=j:
                             e.copy(xb_f[:, j, ti * 128:(ti + 1) * 128], pA[:, j * 128:(j + 1) * 128]),
                             reads=[pA], writes=[xb_f], inc=(j == 7))
                for (w_l, cdt, dst_s, park, sname) in (
                        (wq_l, qcd, q_s, q_park, s_qf),
                        (wk_l, kcd, k_s, k_park, s_kf),
                        (wv_l, vcd, v_s, v_park, s_vf)):
                    isv = sname is s_vf
                    for m in range(8):
                        pp = pB if m % 2 == 0 else pB2
                        for k in range(8):
                            p.op("pe", lambda e, pp=pp, w_l=w_l, k=k, m=m:
                                 e.matmul(pp[:], w_l[:, k, m * 128:(m + 1) * 128], xb_f[:, k, :],
                                          start=(k == 0), stop=(k == 7)),
                                 reads=[w_l, xb_f], writes=[pp], inc=(k == 7))
                        p.op("act", lambda e, pp=pp, dst_s=dst_s, m=m:
                             e.copy(dst_s[:, m, 3:3 + S1], pp[:]), reads=[pp], writes=[dst_s])
                    for m in range(8):
                        pp = pC if m % 2 == 0 else pC2
                        for k in range(4):
                            p.op("pe", lambda e, pp=pp, cdt=cdt, dst_s=dst_s, m=m, k=k:
                                 e.matmul(pp[:], cdt[:, (m * 4 + k) * 128:(m * 4 + k + 1) * 128],
                                          dst_s[:, m, 3 - k:3 - k + S1], start=(k == 0), stop=(k == 3)),
                                 reads=[cdt, dst_s], writes=[pp], inc=(k == 3))
                        p.op("act", lambda e, pp=pp, park=park, m=m:
                             e.activation(park[:, m, :], pp[:], AF.Silu), reads=[pp], writes=[park])
                        if isv:
                            p.op("act", lambda e, pp=pp, m=m:
                                 e.activation(v_f32[:], pp[:], AF.Silu), reads=[pp], writes=[v_f32])
                            for ti in range(4):
                                p.op("pe", lambda e, ti=ti:
                                     e.transpose(pE[:, (ti % 4) * 128:(ti % 4) * 128 + 128],
                                                 v_f32[:, ti * 128:(ti + 1) * 128], identf[:]),
                                     reads=[v_f32, identf], writes=[pE], inc=(ti == 3))
                            for ti in range(4):
                                p.op("act", lambda e, ti=ti, m=m:
                                     e.copy(v_tok[:, ti, m * 128:(m + 1) * 128], pE[:, ti * 128:ti * 128 + 128]),
                                     reads=[pE], writes=[v_tok], inc=(ti == 3))
                    for m in range(8):
                        p.op("act", lambda e, dst_s=dst_s, m=m:
                             e.copy(dst_s[:, m, 0:3], dst_s[:, m, S1:S1 + 3]), reads=[dst_s], writes=[dst_s],
                             inc=(m == 7))
                    if isv:
                        p.dma("dact", sname[:, :, 128 + t0:128 + t0 + S1].rearrange("m p t -> p m t"),
                              park[:])
                    else:
                        p.dma("dact", sname[:, :, t0:t0 + S1].rearrange("m p t -> p m t"),
                              park[:])
                p.dma("dact", s_vtok[t0:t0 + S1, :].rearrange("(ti p) d -> p ti d", p=128),
                      v_tok[:])
                for k in range(8):
                    p.op("pe", lambda e, k=k: e.matmul(pD[:], wb_l[:, k, :], xb_f[:, k, :],
                                                       start=(k == 0), stop=(k == 7)),
                         reads=[wb_l, xb_f], writes=[pD], inc=(k == 7))
                p.op("act", lambda e: e.copy(blog_s[:], pD[:]), reads=[pD], writes=[blog_s])
                p.dma("dact", s_blog[:, t0:t0 + S1], blog_s[:])

        p.barrier()

        if max_phase < 2:
            p.barrier_dma_done()
            p.emit()
            return nc
        with contextlib.ExitStack() as s2:
            sb2 = lambda name, shape, dt: s2.enter_context(nc.sbuf_tensor(name, shape, dt))
            ps2_ = lambda name, shape, dt: s2.enter_context(nc.psum_tensor(name, shape, dt))

            blog_sb = sb2("blog_sb", [H, T], F32)
            btok = sb2("btok", [128, NCH * H], F32)
            nbtok = sb2("nbtok", [128, NCH * H], F32)
            S_f = sb2("S_f", [128, H * 2 * DH], F32)
            S_b = sb2("S_b", [128, H * 2 * DH], BF16)
            identb4 = sb2("identb4", [128, 512], BF16)
            qf_c = [sb2(f"qf_c{i}", [128, 8, CH], BF16) for i in range(2)]
            kf_c = [sb2(f"kf_c{i}", [128, 8, CH], BF16) for i in range(2)]
            vt_c = [sb2(f"vt_c{i}", [128, D], F32) for i in range(2)]
            dpark = [sb2(f"dpark{i}", [128, D], F32) for i in range(2)]
            qtokB = [sb2(f"qtok{i}", [128, D], F32) for i in range(2)]
            ktokB = [sb2(f"ktok{i}", [128, D], F32) for i in range(2)]
            qnB = [sb2(f"qn{i}", [128, D], BF16) for i in range(2)]
            knB = [sb2(f"kn{i}", [128, D], BF16) for i in range(2)]
            qnfB = [sb2(f"qnf{i}", [128, 8, CH], BF16) for i in range(2)]
            knfB = [sb2(f"knf{i}", [128, 8, CH], BF16) for i in range(2)]
            kbB = [sb2(f"kb{i}", [128, D], BF16) for i in range(2)]
            vbB = [sb2(f"vb{i}", [128, D], BF16) for i in range(2)]
            HmB = [sb2(f"Hm{i}", [128, 512], BF16) for i in range(2)]
            GmB = [sb2(f"Gm{i}", [128, 512], BF16) for i in range(2)]
            EmB = [sb2(f"Em{i}", [128, 512], BF16) for i in range(2)]
            AttTB = [sb2(f"AttT{i}", [128, 512], BF16) for i in range(2)]
            usbB = [sb2(f"usb{i}", [128, D], F32) for i in range(2)]
            wfB = [sb2(f"wf{i}", [128, 8, CH], BF16) for i in range(2)]
            uaB = [sb2(f"ua{i}", [128, D], F32) for i in range(2)]
            uabB = [sb2(f"uab{i}", [128, D], BF16) for i in range(2)]
            ssB = [sb2(f"ss{i}", [128, 24], F32) for i in range(2)]
            sqscrP = [sb2(f"sqscrP{i}", [128, DH], BF16) for i in range(2)]

            pT1 = ps2_("pT1", [128, 1024], BF16)
            pT2 = ps2_("pT2", [128, 1024], BF16)
            pP3 = ps2_("pP3", [128, 512], F32)
            pP4 = ps2_("pP4", [128, 512], F32)
            pP5 = ps2_("pP5", [128, 512], F32)
            pS1 = ps2_("pS1", [128, 512], F32)
            pS2b = ps2_("pS2b", [128, 512], F32)
            pS3 = ps2_("pS3", [128, 512], F32)

            p.dma("dsp", blog_sb[:], s_blog)
            p.op("act", lambda e: e.activation(blog_sb[:], blog_sb[:], AF.Sigmoid),
                 reads=[blog_sb], writes=[blog_sb])
            for c in range(NCH):
                p.op("pe", lambda e, c=c: e.transpose(pP3[:, 0:H], blog_sb[0:H, c * CH:(c + 1) * CH],
                                                      identf[0:H, 0:H]),
                     reads=[blog_sb, identf], writes=[pP3])
                p.op("act", lambda e, c=c: e.copy(btok[:, c * H:(c + 1) * H], pP3[:, 0:H]),
                     reads=[pP3], writes=[btok])
            p.op("dve", lambda e: e.tensor_scalar(nbtok[:], btok[:], -1.0, None, ALU.mult, ALU.bypass),
                 reads=[btok], writes=[nbtok])
            p.op("pool", lambda e: e.memset(S_f[:], 0.0), writes=[S_f])
            p.op("pool", lambda e: e.memset(S_b[:], 0.0), writes=[S_b])
            for h in range(H):
                p.op("dve", lambda e, h=h: e.tensor_copy(identb4[:, h * 128:(h + 1) * 128], identb[:]),
                     reads=[identb], writes=[identb4])

            NFAC = 7

            def p2_prep(c, qf, kf, vt, qtok, ktok, qn, kn, qnf, knf, kb, vb,
                        Hm, Gm, Em, AttT, usb, wf, ss, sqp):
                t0 = c * CH
                p.dma("dsp", qf[:], s_qf[:, :, t0:t0 + CH].rearrange("m p t -> p m t"))
                p.dma("dsp", kf[:], s_kf[:, :, t0:t0 + CH].rearrange("m p t -> p m t"))
                p.dma("dsp", vt[:], s_vtok[t0:t0 + CH, :])
                for h in range(H):
                    for j in range(2):
                        p.op("pe", lambda e, h=h, j=j, qf=qf:
                             e.transpose(pT1[:, h * DH + j * CH:h * DH + (j + 1) * CH], qf[:, 2 * h + j, :], identb[:]),
                             reads=[qf, identb], writes=[pT1], inc=(h == H - 1 and j == 1))
                for h in range(H):
                    for j in range(2):
                        p.op("pe", lambda e, h=h, j=j, kf=kf:
                             e.transpose(pT2[:, h * DH + j * CH:h * DH + (j + 1) * CH], kf[:, 2 * h + j, :], identb[:]),
                             reads=[kf, identb], writes=[pT2], inc=(h == H - 1 and j == 1))
                p.op("act", lambda e, qtok=qtok: e.copy(qtok[:], pT1[:, 0:D]), reads=[pT1], writes=[qtok])
                p.op("dve", lambda e, ktok=ktok: e.tensor_copy(ktok[:], pT2[:, 0:D]), reads=[pT2], writes=[ktok])
                for h in range(H):
                    p.op("dve", lambda e, ss=ss, h=h: e.memset(ss[:, h:h + 1], 0.0), writes=[ss])
                    p.op("act", lambda e, qtok=qtok, ss=ss, h=h, sqp=sqp:
                         e.activation(sqp[:], qtok[:, h * DH:(h + 1) * DH], AF.Square, accum_out=ss[:, h:h + 1]),
                         reads=[qtok], writes=[sqp, ss])
                    p.op("dve", lambda e, ss=ss, h=h: e.memset(ss[:, 4 + h:5 + h], 0.0), writes=[ss])
                    p.op("act", lambda e, ktok=ktok, ss=ss, h=h, sqp=sqp:
                         e.activation(sqp[:], ktok[:, h * DH:(h + 1) * DH], AF.Square, accum_out=ss[:, 4 + h:5 + h]),
                         reads=[ktok], writes=[sqp, ss])
                p.op("act", lambda e, ss=ss: e.activation(ss[:, 8:16], ss[:, 0:8], AF.Ln),
                     reads=[ss], writes=[ss])
                p.op("act", lambda e, ss=ss: e.activation(ss[:, 16:24], ss[:, 8:16], AF.Exp, scale=-0.5),
                     reads=[ss], writes=[ss])
                for h in range(H):
                    bcol = btok[:, c * H + h:c * H + h + 1]
                    p.op("dve", lambda e, qn=qn, qtok=qtok, ss=ss, h=h:
                         e.tensor_single_scalar(qn[:, h * DH:(h + 1) * DH], qtok[:, h * DH:(h + 1) * DH],
                                                ss[:, 16 + h:17 + h], ALU.mult),
                         reads=[qtok, ss], writes=[qn])
                    p.op("dve", lambda e, kn=kn, ktok=ktok, ss=ss, h=h:
                         e.tensor_single_scalar(kn[:, h * DH:(h + 1) * DH], ktok[:, h * DH:(h + 1) * DH],
                                                ss[:, 20 + h:21 + h], ALU.mult),
                         reads=[ktok, ss], writes=[kn])
                    p.op("dve", lambda e, kb=kb, kn=kn, bcol=bcol, h=h:
                         e.tensor_single_scalar(kb[:, h * DH:(h + 1) * DH], kn[:, h * DH:(h + 1) * DH], bcol, ALU.mult),
                         reads=[kn, btok], writes=[kb])
                    p.op("dve", lambda e, vb=vb, vt=vt, bcol=bcol, h=h:
                         e.tensor_single_scalar(vb[:, h * DH:(h + 1) * DH], vt[:, h * DH:(h + 1) * DH], bcol, ALU.mult),
                         reads=[vt, btok], writes=[vb])
                for h in range(H):
                    for j in range(2):
                        p.op("pe", lambda e, qn=qn, h=h, j=j:
                             e.transpose(pT1[:, h * DH + j * CH:h * DH + (j + 1) * CH],
                                         qn[:, h * DH + j * CH:h * DH + (j + 1) * CH], identb[:]),
                             reads=[qn, identb], writes=[pT1], inc=(h == H - 1 and j == 1))
                for h in range(H):
                    for j in range(2):
                        p.op("pe", lambda e, kn=kn, h=h, j=j:
                             e.transpose(pT2[:, h * DH + j * CH:h * DH + (j + 1) * CH],
                                         kn[:, h * DH + j * CH:h * DH + (j + 1) * CH], identb[:]),
                             reads=[kn, identb], writes=[pT2], inc=(h == H - 1 and j == 1))
                p.op("act", lambda e, qnf=qnf: e.copy(qnf[:].rearrange("p m t -> p (m t)"), pT1[:, 0:D]),
                     reads=[pT1], writes=[qnf])
                p.op("act", lambda e, knf=knf: e.copy(knf[:].rearrange("p m t -> p (m t)"), pT2[:, 0:D]),
                     reads=[pT2], writes=[knf])
                for h in range(H):
                    for j in range(2):
                        p.op("pe", lambda e, knf=knf, h=h, j=j:
                             e.matmul(pP3[:, h * CH:(h + 1) * CH], knf[:, 2 * h + j, :], knf[:, 2 * h + j, :],
                                      start=(j == 0), stop=(j == 1)),
                             reads=[knf], writes=[pP3], inc=(h == H - 1 and j == 1))
                for h in range(H):
                    p.op("dve", lambda e, Hm=Hm, h=h, c=c:
                         e.scalar_tensor_tensor(Hm[:, h * CH:(h + 1) * CH], pP3[:, h * CH:(h + 1) * CH],
                                                nbtok[:, c * H + h:c * H + h + 1], mask_lt[:], ALU.mult, ALU.mult),
                         reads=[pP3, nbtok, mask_lt], writes=[Hm], inc=(h == H - 1))
                for h in range(H):
                    for j in range(2):
                        p.op("pe", lambda e, knf=knf, qnf=qnf, h=h, j=j:
                             e.matmul(pP4[:, h * CH:(h + 1) * CH], knf[:, 2 * h + j, :], qnf[:, 2 * h + j, :],
                                      start=(j == 0), stop=(j == 1)),
                             reads=[knf, qnf], writes=[pP4], inc=(h == H - 1 and j == 1))
                for h in range(H):
                    p.op("dve", lambda e, AttT=AttT, h=h:
                         e.tensor_mul(AttT[:, h * CH:(h + 1) * CH], pP4[:, h * CH:(h + 1) * CH], mask_ut[:]),
                         reads=[pP4, mask_ut], writes=[AttT], inc=(h == H - 1))
                for h in range(H):
                    p.op("pe", lambda e, Hm=Hm, h=h:
                         e.transpose(pT2[:, h * CH:(h + 1) * CH], Hm[:, h * CH:(h + 1) * CH], identb[:]),
                         reads=[Hm, identb], writes=[pT2], inc=(h == H - 1))
                p.op("act", lambda e, Gm=Gm: e.copy(Gm[:], pT2[:, 0:512]), reads=[pT2], writes=[Gm])
                p.op("dve", lambda e, Em=Em, Gm=Gm: e.tensor_add(Em[:], Gm[:], identb4[:]),
                     reads=[Gm, identb4], writes=[Em])
                for jj in range(1, NFAC):
                    for h in range(H):
                        p.op("pe", lambda e, Gm=Gm, Hm=Hm, h=h:
                             e.matmul(pP3[:, h * CH:(h + 1) * CH], Gm[:, h * CH:(h + 1) * CH],
                                      Hm[:, h * CH:(h + 1) * CH], start=True, stop=True),
                             reads=[Gm, Hm], writes=[pP3], inc=(h == H - 1))
                    if jj < NFAC - 1:
                        for h in range(H):
                            p.op("pe", lambda e, Gm=Gm, Hm=Hm, h=h:
                                 e.matmul(pP4[:, h * CH:(h + 1) * CH], Hm[:, h * CH:(h + 1) * CH],
                                          Gm[:, h * CH:(h + 1) * CH], start=True, stop=True),
                                 reads=[Hm, Gm], writes=[pP4], inc=(h == H - 1))
                        p.op("dve", lambda e, Gm=Gm: e.tensor_copy(Gm[:], pP4[:, 0:512]),
                             reads=[pP4], writes=[Gm])
                    p.op("act", lambda e, Hm=Hm: e.copy(Hm[:], pP3[:, 0:512]), reads=[pP3], writes=[Hm])
                    for h in range(H):
                        p.op("pe", lambda e, Em=Em, h=h:
                             e.matmul(pP5[:, h * CH:(h + 1) * CH], identb[:], Em[:, h * CH:(h + 1) * CH],
                                      start=True, stop=False),
                             reads=[identb, Em], writes=[pP5], inc=False)
                        p.op("pe", lambda e, Em=Em, Hm=Hm, h=h:
                             e.matmul(pP5[:, h * CH:(h + 1) * CH], Hm[:, h * CH:(h + 1) * CH],
                                      Em[:, h * CH:(h + 1) * CH], start=False, stop=True),
                             reads=[Hm, Em], writes=[pP5], inc=(h == H - 1))
                    p.op("dve", lambda e, Em=Em: e.tensor_copy(Em[:], pP5[:, 0:512]), reads=[pP5], writes=[Em])
                for h in range(H):
                    pb = pP3 if h < 2 else pP4
                    p.op("pe", lambda e, Em=Em, vb=vb, pb=pb, h=h:
                         e.matmul(pb[:, (h % 2) * DH:(h % 2 + 1) * DH], Em[:, h * CH:(h + 1) * CH],
                                  vb[:, h * DH:(h + 1) * DH], start=True, stop=True),
                         reads=[Em, vb], writes=[pb], inc=(h % 2 == 1))
                p.op("dve", lambda e, usb=usb: e.tensor_copy(usb[:, 0:512], pP3[:, 0:512]),
                     reads=[pP3], writes=[usb])
                p.op("dve", lambda e, usb=usb: e.tensor_copy(usb[:, 512:1024], pP4[:, 0:512]),
                     reads=[pP4], writes=[usb])
                for h in range(H):
                    pb = pP3 if h < 2 else pP4
                    p.op("pe", lambda e, Em=Em, kb=kb, pb=pb, h=h:
                         e.matmul(pb[:, (h % 2) * DH:(h % 2 + 1) * DH], Em[:, h * CH:(h + 1) * CH],
                                  kb[:, h * DH:(h + 1) * DH], start=True, stop=True),
                         reads=[Em, kb], writes=[pb], inc=(h % 2 == 1))
                p.op("act", lambda e, qn=qn: e.copy(qn[:, 0:512], pP3[:, 0:512]), reads=[pP3], writes=[qn])
                p.op("act", lambda e, qn=qn: e.copy(qn[:, 512:1024], pP4[:, 0:512]), reads=[pP4], writes=[qn])
                for h in range(H):
                    for j in range(2):
                        p.op("pe", lambda e, qn=qn, h=h, j=j:
                             e.transpose(pT1[:, h * DH + j * CH:h * DH + (j + 1) * CH],
                                         qn[:, h * DH + j * CH:h * DH + (j + 1) * CH], identb[:]),
                             reads=[qn, identb], writes=[pT1], inc=(h == H - 1 and j == 1))
                p.op("act", lambda e, wf=wf: e.copy(wf[:].rearrange("p m t -> p (m t)"), pT1[:, 0:D]),
                     reads=[pT1], writes=[wf])

            def p2_scan(c, pc):
                t0 = c * CH
                qnf, knf, kn, AttT, usb, wf = qnfB[pc], knfB[pc], knB[pc], AttTB[pc], usbB[pc], wfB[pc]
                ua, uab = uaB[pc], uabB[pc]
                for h in range(H):
                    pb = pS1 if h < 2 else pS2b
                    for j in range(2):
                        p.op("pe", lambda e, wf=wf, pb=pb, h=h, j=j:
                             e.matmul(pb[:, (h % 2) * DH:(h % 2 + 1) * DH], wf[:, 2 * h + j, :],
                                      S_b[:, (h * 2 + j) * DH:(h * 2 + j + 1) * DH],
                                      start=(j == 0), stop=(j == 1)),
                             reads=[wf, S_b], writes=[pb], inc=(h % 2 == 1 and j == 1))
                p.op("dve", lambda e, ua=ua, usb=usb: e.tensor_sub(ua[:, 0:512], usb[:, 0:512], pS1[:, 0:512]),
                     reads=[usb, pS1], writes=[ua])
                p.op("dve", lambda e, ua=ua, usb=usb: e.tensor_sub(ua[:, 512:1024], usb[:, 512:1024], pS2b[:, 0:512]),
                     reads=[usb, pS2b], writes=[ua])
                p.op("act", lambda e, uab=uab, ua=ua: e.copy(uab[:], ua[:]), reads=[ua], writes=[uab])
                for h in range(H):
                    pb = pS1 if h < 2 else pS2b
                    sl = slice((h % 2) * DH, (h % 2 + 1) * DH)
                    for j in range(2):
                        p.op("pe", lambda e, qnf=qnf, pb=pb, sl=sl, h=h, j=j:
                             e.matmul(pb[:, sl], qnf[:, 2 * h + j, :],
                                      S_b[:, (h * 2 + j) * DH:(h * 2 + j + 1) * DH], start=(j == 0), stop=False),
                             reads=[qnf, S_b], writes=[pb], inc=False)
                    p.op("pe", lambda e, AttT=AttT, uab=uab, pb=pb, sl=sl, h=h:
                         e.matmul(pb[:, sl], AttT[:, h * CH:(h + 1) * CH], uab[:, h * DH:(h + 1) * DH],
                                  start=False, stop=True),
                         reads=[AttT, uab], writes=[pb], inc=(h % 2 == 1))
                p.op("dve", lambda e, pc=pc: e.tensor_copy(dpark[pc][:, 0:512], pS1[:, 0:512]),
                     reads=[pS1], writes=[dpark[pc]])
                p.op("dve", lambda e, pc=pc: e.tensor_copy(dpark[pc][:, 512:1024], pS2b[:, 0:512]),
                     reads=[pS2b], writes=[dpark[pc]])
                p.dma("dact", s_delta[t0:t0 + CH, :], dpark[pc][:])
                for wave in range(2):
                    for hh in range(2):
                        h = wave * 2 + hh
                        pb = pS1 if hh == 0 else pS2b
                        for j in range(2):
                            p.op("pe", lambda e, kn=kn, uab=uab, pb=pb, h=h, j=j:
                                 e.matmul(pb[:, j * DH:(j + 1) * DH], kn[:, h * DH + j * CH:h * DH + (j + 1) * CH],
                                          uab[:, h * DH:(h + 1) * DH], start=True, stop=True),
                                 reads=[kn, uab], writes=[pb], inc=(j == 1))
                    for hh in range(2):
                        h = wave * 2 + hh
                        pb = pS1 if hh == 0 else pS2b
                        p.op("dve", lambda e, pb=pb, h=h:
                             e.tensor_add(S_f[:, h * 2 * DH:(h + 1) * 2 * DH], S_f[:, h * 2 * DH:(h + 1) * 2 * DH],
                                          pb[:, 0:512]),
                             reads=[S_f, pb], writes=[S_f])
                    p.op("act", lambda e, wave=wave:
                         e.copy(S_b[:, wave * 1024:(wave + 1) * 1024], S_f[:, wave * 1024:(wave + 1) * 1024]),
                         reads=[S_f], writes=[S_b])

            def do_prep(c):
                pc = c % 2
                p2_prep(c, qf_c[pc], kf_c[pc], vt_c[pc], qtokB[pc], ktokB[pc], qnB[pc], knB[pc],
                        qnfB[pc], knfB[pc], kbB[pc], vbB[pc], HmB[pc], GmB[pc], EmB[pc], AttTB[pc],
                        usbB[pc], wfB[pc], ssB[pc], sqscrP[pc])

            do_prep(0)
            for c in range(NCH):
                if c + 1 < NCH:
                    do_prep(c + 1)
                p2_scan(c, c % 2)

        p.barrier()

        if max_phase < 3:
            p.barrier_dma_done()
            p.emit()
            return nc
        with contextlib.ExitStack() as s3:
            sb3 = lambda name, shape, dt: s3.enter_context(nc.sbuf_tensor(name, shape, dt))
            ps3_ = lambda name, shape, dt: s3.enter_context(nc.psum_tensor(name, shape, dt))

            firLd = sb3("firLd", [128, 16, 64], F32)
            firSd = sb3("firSd", [128, 16, 5], F32)
            firL_lhs = sb3("firL_lhs", [128, 16 * 16 * 128], BF16)
            firS_lhs = sb3("firS_lhs", [128, 16 * 2 * 128], BF16)

            for g in range(16):
                ct, half = g // 2, (g % 2) * 64
                p.dma("dsp", firLd[0:64, g, :], firL_h[ct * 128 + half:ct * 128 + half + 64, :])
                p.dma("dsp", firLd[64:128, g, :], firL_h[ct * 128 + half:ct * 128 + half + 64, :])
                p.dma("dsp", firSd[0:64, g, :], firS_h[ct * 128 + half:ct * 128 + half + 64, :])
                p.dma("dsp", firSd[64:128, g, :], firS_h[ct * 128 + half:ct * 128 + half + 64, :])
            p.op("pool", lambda e: e.memset(firS_lhs[:], 0.0), writes=[firS_lhs])
            for g in range(16):
                for bb in range(16):
                    o = (g * 16 + bb) * 128
                    for sg in range(2):
                        for ss in range(2):
                            kk = 4 * bb + 2 * sg + ss
                            p.op("dve", lambda e, o=o, sg=sg, ss=ss, g=g, kk=kk:
                                 e.tensor_scalar(firL_lhs[ss * 64:(ss + 1) * 64, o + sg * 64:o + sg * 64 + 64],
                                                 did2[ss * 64:(ss + 1) * 64, :],
                                                 firLd[ss * 64:(ss + 1) * 64, g, 63 - kk:64 - kk],
                                                 None, ALU.mult, ALU.bypass),
                                 reads=[did2, firLd], writes=[firL_lhs], inc=(sg == 1 and ss == 1))
                for bb in range(2):
                    o = (g * 2 + bb) * 128
                    for sg in range(2):
                        for ss in range(2):
                            kk = 4 * bb + 2 * sg + ss
                            if kk >= 5:
                                continue
                            p.op("dve", lambda e, o=o, sg=sg, ss=ss, g=g, kk=kk:
                                 e.tensor_scalar(firS_lhs[ss * 64:(ss + 1) * 64, o + sg * 64:o + sg * 64 + 64],
                                                 did2[ss * 64:(ss + 1) * 64, :],
                                                 firSd[ss * 64:(ss + 1) * 64, g, 4 - kk:5 - kk],
                                                 None, ALU.mult, ALU.bypass),
                                 reads=[did2, firSd], writes=[firS_lhs], inc=True)

            HLO = 66
            vdup = [sb3(f"vdup{i}", [128, HLO + S3], BF16) for i in range(4)]
            scrL = [sb3(f"scrL{i}", [128, S3 + 2], F32) for i in range(2)]
            scrS = [sb3(f"scrS{i}", [128, S3 + 2], F32) for i in range(2)]
            ls_tok = sb3("ls_tok", [128, 2, D], F32)
            ll_tok = sb3("ll_tok", [128, 2, D], F32)

            pF = [ps3_(f"pF{i}", [128, 512], F32) for i in range(2)]
            pFS = [ps3_(f"pFS{i}", [128, 512], F32) for i in range(2)]
            pY0 = ps3_("pY0", [128, 512], F32)
            pY1 = ps3_("pY1", [128, 512], F32)
            pY0b = ps3_("pY0b", [128, 512], F32)
            pY1b = ps3_("pY1b", [128, 512], F32)
            cmb_scr = sb3("cmb_scr", [128, 256], F32)
            cmb_scrb = sb3("cmb_scrb", [128, 256], F32)

            def fir_combine(scrpair, dst_t, ct, pY0, pY1, cmb_scr):
                for ti in range(2):
                    for gg in range(2):
                        o = ti * 128 + gg * 64
                        sc = scrpair[gg]
                        p.op("pe", lambda e, sc=sc, o=o, ti=ti, pY0=pY0:
                             e.matmul(pY0[:, o:o + 64], sc[0:64, 2 + ti * 128:2 + ti * 128 + 128],
                                      identf[0:64, 0:64], start=True, stop=True),
                             reads=[sc, identf], writes=[pY0], inc=(gg == 1))
                        p.op("pe", lambda e, sc=sc, o=o, ti=ti, pY1=pY1:
                             e.matmul(pY1[:, o:o + 64], sc[64:128, ti * 128:ti * 128 + 128],
                                      identf[64:128, 64:128], start=True, stop=True),
                             reads=[sc, identf], writes=[pY1], inc=(gg == 1))
                p.op("act", lambda e, pY0=pY0, cmb_scr=cmb_scr: e.copy(cmb_scr[:], pY0[:, 0:256]),
                     reads=[pY0], writes=[cmb_scr])
                for ti in range(2):
                    p.op("dve", lambda e, dst_t=dst_t, ct=ct, ti=ti, pY1=pY1, cmb_scr=cmb_scr:
                         e.tensor_add(dst_t[:, ti, ct * 128:(ct + 1) * 128],
                                      cmb_scr[:, ti * 128:(ti + 1) * 128], pY1[:, ti * 128:(ti + 1) * 128]),
                         reads=[cmb_scr, pY1], writes=[dst_t])

            for s in range(p3a_segs):
                t0 = s * S3
                for ct in range(8):
                    for gg in range(2):
                        g = ct * 2 + gg
                        half = gg * 64
                        vd = vdup[(ct % 2) * 2 + gg]
                        vq = "dsp" if gg == 0 else "dpool"
                        p.dma(vq, vd[0:64, :], s_vf[ct, half:half + 64, 128 + t0 - HLO:128 + t0 + S3])
                        p.dma(vq, vd[64:128, 1:HLO + S3],
                              s_vf[ct, half:half + 64, 128 + t0 - HLO:128 + t0 + S3 - 1])
                        if p3a_stage < 2:
                            continue
                        for bb in range(16):
                            p.op("pe", lambda e, vd=vd, g=g, bb=bb, gg=gg:
                                 e.matmul(pF[gg][:, 0:S3 + 2], firL_lhs[:, (g * 16 + bb) * 128:(g * 16 + bb + 1) * 128],
                                          vd[:, HLO - 4 * bb - 2:HLO - 4 * bb + S3], start=(bb == 0), stop=(bb == 15)),
                                 reads=[firL_lhs, vd], writes=[pF[gg]], inc=(bb == 15))
                        for bb in range(2):
                            p.op("pe", lambda e, vd=vd, g=g, bb=bb, gg=gg:
                                 e.matmul(pFS[gg][:, 0:S3 + 2], firS_lhs[:, (g * 2 + bb) * 128:(g * 2 + bb + 1) * 128],
                                          vd[:, HLO - 4 * bb - 2:HLO - 4 * bb + S3], start=(bb == 0), stop=(bb == 1)),
                                 reads=[firS_lhs, vd], writes=[pFS[gg]], inc=(bb == 1))
                        p.op("act", lambda e, gg=gg: e.copy(scrL[gg][:], pF[gg][:, 0:S3 + 2]),
                             reads=[pF[gg]], writes=[scrL[gg]])
                        p.op("act", lambda e, gg=gg: e.copy(scrS[gg][:], pFS[gg][:, 0:S3 + 2]),
                             reads=[pFS[gg]], writes=[scrS[gg]])
                    if p3a_stage < 3:
                        continue
                    fir_combine(scrL, ll_tok, ct, pY0, pY1, cmb_scr)
                    fir_combine(scrS, ls_tok, ct, pY0b, pY1b, cmb_scrb)
                p.dma("dact", s_ls[t0:t0 + S3, :].rearrange("(ti p) d -> p ti d", p=128), ls_tok[:])
                p.dma("dact", s_ll[t0:t0 + S3, :].rearrange("(ti p) d -> p ti d", p=128), ll_tok[:])

        p.barrier()
        if max_phase < 3.5:
            p.barrier_dma_done()
            p.emit()
            return nc
        with contextlib.ExitStack() as s3:
            sb3 = lambda name, shape, dt: s3.enter_context(nc.sbuf_tensor(name, shape, dt))
            ps3_ = lambda name, shape, dt: s3.enter_context(nc.psum_tensor(name, shape, dt))

            wo_l = sb3("wo_l", [128, 8, D], BF16)
            w1x_l = sb3("w1x_l", [128, 8, D], BF16)
            w1s_l = sb3("w1s_l", [16, D], BF16)
            w2g_l = sb3("w2g_l", [128, 8, H], BF16)
            fb1v = sb3("fb1v", [128, 8], F32)
            if p3b_weights:
                with contextlib.ExitStack() as s3w:
                    wo_raw = s3w.enter_context(nc.sbuf_tensor("wo_raw", [128, 8, D], BF16))
                    p.dma("dpool", wo_raw[:], wo_h.rearrange("(k p) m -> p k m", p=128))
                    for j in range(8):
                        if skip_fold:
                            p.op("dve", lambda e, j=j: e.tensor_copy(wo_l[:, j, :], wo_raw[:, j, :]),
                                 reads=[wo_raw], writes=[wo_l])
                        else:
                            p.op("dve", lambda e, j=j: e.tensor_single_scalar(wo_l[:, j, :], wo_raw[:, j, :],
                                                                              onorm_s[:, j % 2:j % 2 + 1], ALU.mult),
                                 reads=[wo_raw, onorm_s], writes=[wo_l])
                p.barrier()
                p.dma("dpool", w1x_l[:], w1x_h.rearrange("(k p) m -> p k m", p=128))
                p.dma("dpool", w1s_l[:], w1s_h)
                p.dma("dpool", w2g_l[:], w2g_h.rearrange("(k p) m -> p k m", p=128))
                p.dma("dsp", fb1v[:], fb1_h)

            ls_tokB = [sb3(f"ls_tok3b{i}", [128, 2, D], F32) for i in range(2)]
            ll_tokB = [sb3(f"ll_tok3b{i}", [128, 2, D], F32) for i in range(2)]
            dl_tokB = [sb3(f"dl_tok{i}", [128, 2, D], F32) for i in range(2)]
            vv_tokB = [sb3(f"vv_tok{i}", [128, 2, D], F32) for i in range(2)]
            x_tok3B = [sb3(f"x_tok3{i}", [128, 2, D], F32) for i in range(3)]
            xb_tok3B = [sb3(f"xb_tok3{i}", [128, 2, D], BF16) for i in range(2)]
            xb_f3B = [sb3(f"xb_f3{i}", [128, 8, S3], BF16) for i in range(2)]
            sums3B = [sb3(f"sums3{i}", [128, 2, H, 12], F32) for i in range(2)]
            stats3B = [sb3(f"stats3{i}", [128, 2, H, 16], F32) for i in range(2)]
            statsTB = [sb3(f"statsT{i}", [16, H, S3], BF16) for i in range(2)]
            sqscr3B = [sb3(f"sqscr3{i}", [128, D], F32) for i in range(2)]
            sqBB = [sb3(f"sqB{i}", [128, D], F32) for i in range(2)]
            sumsAbB = [sb3(f"sumsAb{i}", [128, 2, H, 4], F32) for i in range(2)]
            rms_scB = [sb3(f"rms_sc{i}", [128, 2, H, 4], F32) for i in range(2)]
            ln_sc = sb3("ln_sc", [128, 2, 8], F32)
            tmp3bB = [sb3(f"tmp3b{i}", [128, 2, H, 4], F32) for i in range(2)]
            hdnB = [sb3(f"hdn{i}", [128, 8, S3], BF16) for i in range(2)]
            lg3B = [sb3(f"lg3{i}", [128, 2, H, 4], F32) for i in range(2)]
            coefB = [sb3(f"coef{i}", [128, 2, H, 4], F32) for i in range(2)]
            sc3B = [sb3(f"sc3{i}", [128, 3, H, 4], F32) for i in range(2)]
            o_tok_s = [sb3(f"o_at{t}", [128, D], F32) for t in range(2)]
            o_tokB = [o_tok_s, o_tok_s]
            o_nbB = [sb3(f"o_nb{i}", [128, 2, D], BF16) for i in range(2)]
            o_fB = [sb3(f"o_f{i}", [128, 8, S3], BF16) for i in range(2)]
            xa_s = sb3("xa", [128, 2, D], F32)
            xaB = [xa_s, xa_s]
            x2pB = [xa_s, xa_s]

            pTT = ps3_("pTT3b", [128, 512], F32)
            pTb = ps3_("pTb", [128, 1024], BF16)
            pTb2 = ps3_("pTb2", [128, 1024], BF16)
            pG3 = ps3_("pG3", [128, 512], F32)
            pL3 = ps3_("pL3", [128, 512], F32)
            pZ3 = ps3_("pZ3", [128, 512], F32)
            pO3 = ps3_("pO3", [128, 512], F32)
            pST = ps3_("pST", [128, 512], F32)

            b2g_it = sb3("b2g_it", [128, H, 4], F32)
            for h in range(H):
                p.op("dve", lambda e, h=h:
                     e.tensor_single_scalar(b2g_it[:, h, :], b2g_b[:, 0:4], itemp_b[:, 0:1], ALU.mult),
                     reads=[b2g_b, itemp_b], writes=[b2g_it])
            if "s_dbg7" in debug_taps:
                p.dma("dpool", s_dbg7.rearrange("(k p) m -> p k m", p=128), wo_l[:])
            def p3b_segA(s, t0, ls_tok, ll_tok, dl_tok, vv_tok, x_tok3, xb_tok3, xb_f3,
                         sums3, stats3, statsT, sqscr3, tmp3b, sumsAb, rms_sc, hdn, lg3, coef, sc3,
                         o_tok, o_nb, o_f, xa, x2p):
                p.dma("dsp", ls_tok[:], s_ls[t0:t0 + S3, :].rearrange("(ti p) d -> p ti d", p=128))
                p.dma("dsp", ll_tok[:], s_ll[t0:t0 + S3, :].rearrange("(ti p) d -> p ti d", p=128))
                p.dma("dact", dl_tok[:], s_delta[t0:t0 + S3, :].rearrange("(ti p) d -> p ti d", p=128))
                p.dma("dact", vv_tok[:], s_vtok[t0:t0 + S3, :].rearrange("(ti p) d -> p ti d", p=128))
                p.dma("dpool", x_tok3[:], x_h[t0:t0 + S3, :].rearrange("(ti p) d -> p ti d", p=128))
                p.dma("dpool", xb_tok3[:], x_h[t0:t0 + S3, :].rearrange("(ti p) d -> p ti d", p=128))
                for ti in range(2):
                    for j in range(8):
                        p.op("pe", lambda e, xb_tok3=xb_tok3, ti=ti, j=j:
                             e.transpose(pTb[:, j * 128:(j + 1) * 128], xb_tok3[:, ti, j * 128:(j + 1) * 128], identb[:]),
                             reads=[xb_tok3, identb], writes=[pTb], inc=(j == 7))
                    p.op("act", lambda e, xb_f3=xb_f3, ti=ti:
                         e.copy(xb_f3[:, :, ti * 128:(ti + 1) * 128], pTb[:, 0:1024].rearrange("p (j c) -> p j c", j=8)),
                         reads=[pTb], writes=[xb_f3])
                comps = (ls_tok, ll_tok, dl_tok, vv_tok)
                for ti in range(2):
                    for ci, comp in enumerate(comps):
                        sqw = sqscr3B[(ti * 4 + ci) % 2]
                        p.op("dve", lambda e, comp=comp, ti=ti, ci=ci:
                             e.reduce_sum(sums3[:, ti, :, ci:ci + 1],
                                          comp[:, ti, :].rearrange("p (h d) -> p h d", h=H), axis=AX.X),
                             reads=[comp], writes=[sums3])
                        p.op("dve", lambda e, comp=comp, ti=ti, ci=ci:
                             e.reduce_sum(sumsAb[:, ti, :, ci:ci + 1],
                                          comp[:, ti, :].rearrange("p (h d) -> p h d", h=H), axis=AX.X,
                                          apply_absolute_value=True),
                             reads=[comp], writes=[sumsAb])
                        p.op("act", lambda e, comp=comp, sqw=sqw, ti=ti, ci=ci:
                             e.activation(sqw[:, 0:D], comp[:, ti, :], AF.Square),
                             reads=[comp], writes=[sqw])
                        p.op("dve", lambda e, sqw=sqw, ti=ti, ci=ci:
                             e.reduce_sum(sums3[:, ti, :, 4 + ci:5 + ci],
                                          sqw[:, 0:D].rearrange("p (h d) -> p h d", h=H), axis=AX.X),
                             reads=[sqw], writes=[sums3])
                    p.op("dve", lambda e, ti=ti:
                         e.tensor_scalar(stats3[:, ti, :, 0:4], sums3[:, ti, :, 0:4], 1.0 / DH, None,
                                         ALU.mult, ALU.bypass), reads=[sums3], writes=[stats3])
                    p.op("dve", lambda e, ti=ti:
                         e.tensor_scalar(stats3[:, ti, :, 8:12], sumsAb[:, ti, :, 0:4], 1.0 / DH, None,
                                         ALU.mult, ALU.bypass), reads=[sumsAb], writes=[stats3])
                    p.op("dve", lambda e, ti=ti:
                         e.tensor_mul(tmp3b[:, ti, :, :], stats3[:, ti, :, 0:4], stats3[:, ti, :, 0:4]),
                         reads=[stats3], writes=[tmp3b])
                    p.op("dve", lambda e, ti=ti:
                         e.scalar_tensor_tensor(stats3[:, ti, :, 4:8], sums3[:, ti, :, 4:8], 1.0 / DH,
                                                tmp3b[:, ti, :, :], ALU.mult, ALU.subtract),
                         reads=[sums3, tmp3b], writes=[stats3])
                    p.op("act", lambda e, ti=ti:
                         e.activation(tmp3b[:, ti, :, :], sums3[:, ti, :, 4:8], AF.Ln),
                         reads=[sums3], writes=[tmp3b])
                    p.op("act", lambda e, ti=ti:
                         e.activation(stats3[:, ti, :, 12:16], tmp3b[:, ti, :, :], AF.Exp, scale=0.5),
                         reads=[tmp3b], writes=[stats3])
            def p3b_segA2(s, t0, ls_tok, ll_tok, dl_tok, vv_tok, x_tok3, xb_tok3, xb_f3,
                          sums3, stats3, statsT, sqscr3, tmp3b, sumsAb, rms_sc, hdn, lg3, coef, sc3,
                          o_tok, o_nb, o_f, xa, x2p):
                for ti in range(2):
                    for h in range(H):
                        p.op("pe", lambda e, stats3=stats3, ti=ti, h=h:
                             e.transpose(pST[0:16, h * 128:(h + 1) * 128], stats3[:, ti, h, :], identf[:]),
                             reads=[stats3, identf], writes=[pST], inc=(h == H - 1))
                    p.op("act", lambda e, stats3=stats3, statsT=statsT, ti=ti:
                         e.copy(statsT[0:16, :, ti * 128:(ti + 1) * 128],
                                pST[0:16, 0:512].rearrange("p (h c) -> p h c", h=H)),
                         reads=[pST], writes=[statsT])
            def p3b_segB1(s, t0, ls_tok, ll_tok, dl_tok, vv_tok, x_tok3, xb_tok3, xb_f3,
                          sums3, stats3, statsT, sqscr3, tmp3b, sumsAb, rms_sc, hdn, lg3, coef, sc3,
                          o_tok, o_nb, o_f, xa, x2p):
                for h in range(H):
                    for mt in range(8):
                        pz = (pL3, pG3, pZ3)[(h * 8 + mt) % 3]
                        p.op("pe", lambda e, pz=pz, statsT=statsT, h=h, mt=mt:
                             e.matmul(pz[:, 0:S3], w1s_l[0:16, mt * 128:(mt + 1) * 128], statsT[0:16, h, :],
                                      start=True, stop=False),
                             reads=[w1s_l, statsT], writes=[pz], inc=False)
                        for k in range(8):
                            p.op("pe", lambda e, pz=pz, xb_f3=xb_f3, mt=mt, k=k:
                                 e.matmul(pz[:, 0:S3], w1x_l[:, k, mt * 128:(mt + 1) * 128], xb_f3[:, k, :],
                                          start=False, stop=(k == 7)),
                                 reads=[w1x_l, xb_f3], writes=[pz], inc=(k == 7))
                        p.op("act", lambda e, pz=pz, hdn=hdn, h=h, mt=mt:
                             e.activation(hdn[:, mt, :], pz[:, 0:S3], AF.Gelu, bias=fb1v[:, mt:mt + 1]),
                             reads=[pz, fb1v], writes=[hdn])
                    for ti in range(2):
                        for k in range(8):
                            p.op("pe", lambda e, hdn=hdn, h=h, ti=ti, k=k:
                                 e.matmul(pTT[:, 256:260], hdn[:, k, ti * 128:(ti + 1) * 128], w2g_l[:, k, :],
                                          start=(k == 0), stop=(k == 7)),
                                 reads=[hdn, w2g_l], writes=[pTT], inc=(k == 7))
                        p.op("act", lambda e, lg3=lg3, h=h, ti=ti:
                             e.copy(lg3[:, ti, h, :], pTT[:, 256:260]),
                             reads=[pTT], writes=[lg3])

            def p3b_segB2(s, t0, ls_tok, ll_tok, dl_tok, vv_tok, x_tok3, xb_tok3, xb_f3,
                          sums3, stats3, statsT, sqscr3, tmp3b, sumsAb, rms_sc, hdn, lg3, coef, sc3,
                          o_tok, o_nb, o_f, xa, x2p):
                for ti in range(2):
                    p.op("dve", lambda e, lg3=lg3, ti=ti:
                         e.scalar_tensor_tensor(lg3[:, ti, :, :], lg3[:, ti, :, :], itemp_b[:, 0:1],
                                                b2g_it[:], ALU.mult, ALU.add),
                         reads=[lg3, itemp_b, b2g_it], writes=[lg3])
                p.op("dve", lambda e, lg3=lg3, sc3=sc3:
                     e.reduce_max(sc3[:, 0, :, 0:1].rearrange("p h f -> p (h f)")
                                  .rearrange("p (h f) -> p h f", f=1),
                                  lg3[:, 0, :, :], axis=AX.X, negate=True),
                     reads=[lg3], writes=[sc3])
                p.op("dve", lambda e, lg3=lg3, sc3=sc3:
                     e.reduce_max(sc3[:, 1, :, 0:1].rearrange("p h f -> p (h f)")
                                  .rearrange("p (h f) -> p h f", f=1),
                                  lg3[:, 1, :, :], axis=AX.X, negate=True),
                     reads=[lg3], writes=[sc3])
                for ti in range(2):
                    p.op("dve", lambda e, lg3=lg3, sc3=sc3, ti=ti:
                         e.tensor_add(lg3[:, ti, :, :], lg3[:, ti, :, :],
                                      sc3[:, ti, :, 0:1].broadcast_to([128, H, 4])),
                         reads=[lg3, sc3], writes=[lg3])
                p.op("act", lambda e, lg3=lg3:
                     e.activation(lg3[:].rearrange("p ti h f -> p (ti h f)"),
                                  lg3[:].rearrange("p ti h f -> p (ti h f)"), AF.Exp),
                     reads=[lg3], writes=[lg3])
                for ti in range(2):
                    p.op("dve", lambda e, lg3=lg3, sc3=sc3, ti=ti:
                         e.reduce_sum(sc3[:, ti, :, 1:2].rearrange("p h f -> p (h f)")
                                      .rearrange("p (h f) -> p h f", f=1),
                                      lg3[:, ti, :, :], axis=AX.X),
                         reads=[lg3], writes=[sc3])
                p.op("dve", lambda e, sc3=sc3:
                     e.reciprocal(sc3[:, :, :, 2:3].rearrange("p ti h f -> p (ti h f)"),
                                  sc3[:, :, :, 1:2].rearrange("p ti h f -> p (ti h f)")),
                     reads=[sc3], writes=[sc3])
                for ti in range(2):
                    p.op("dve", lambda e, lg3=lg3, coef=coef, sc3=sc3, ti=ti:
                         e.tensor_mul(coef[:, ti, :, :], lg3[:, ti, :, :],
                                      sc3[:, ti, :, 2:3].broadcast_to([128, H, 4])),
                         reads=[lg3, sc3], writes=[coef])
                for ti in range(2):
                    for h in range(H):
                        p.op("dve", lambda e, coef=coef, h=h, ti=ti:
                             e.tensor_add(coef[:, ti, h, 0:1], coef[:, ti, h, 0:1], rs_b[:]),
                             reads=[coef, rs_b], writes=[coef])
                        p.op("dve", lambda e, coef=coef, h=h, ti=ti:
                             e.tensor_add(coef[:, ti, h, 1:2], coef[:, ti, h, 1:2], rl_b[:]),
                             reads=[coef, rl_b], writes=[coef])
                oa = o_tok[0]
                for h in range(H):
                    p.op("dve", lambda e, oa=oa, h=h:
                         e.tensor_single_scalar(oa[:, h * DH:(h + 1) * DH], ls_tok[:, 0, h * DH:(h + 1) * DH],
                                                coef[:, 0, h, 0:1], ALU.mult),
                         reads=[ls_tok, coef], writes=[oa])
                    for ci, comp in ((1, ll_tok), (2, dl_tok), (3, vv_tok)):
                        p.op("dve", lambda e, oa=oa, h=h, ci=ci, comp=comp:
                             e.scalar_tensor_tensor(oa[:, h * DH:(h + 1) * DH],
                                                    comp[:, 0, h * DH:(h + 1) * DH],
                                                    coef[:, 0, h, ci:ci + 1],
                                                    oa[:, h * DH:(h + 1) * DH], ALU.mult, ALU.add),
                             reads=[comp, coef, oa], writes=[oa])
                oa1 = o_tok[1]
                mixtmp = sqBB[1]
                p.op("pool", lambda e, oa1=oa1:
                     e.tensor_mul(oa1[:].rearrange("p (h d) -> p h d", h=H),
                                  ls_tok[:, 1, :].rearrange("p (h d) -> p h d", h=H),
                                  coef[:, 1, :, 0:1].broadcast_to([128, H, DH])),
                     reads=[ls_tok, coef], writes=[oa1])
                for ci, comp in ((1, ll_tok), (2, dl_tok), (3, vv_tok)):
                    p.op("pool", lambda e, mixtmp=mixtmp, ci=ci, comp=comp:
                         e.tensor_mul(mixtmp[:, 0:D].rearrange("p (h d) -> p h d", h=H),
                                      comp[:, 1, :].rearrange("p (h d) -> p h d", h=H),
                                      coef[:, 1, :, ci:ci + 1].broadcast_to([128, H, DH])),
                         reads=[comp, coef], writes=[mixtmp])
                    p.op("pool", lambda e, oa1=oa1, mixtmp=mixtmp:
                         e.tensor_add(oa1[:], oa1[:], mixtmp[:, 0:D]),
                         reads=[mixtmp, oa1], writes=[oa1])
                for ti in range(2):
                    oa = o_tok[ti]
                    sqw = sqBB[ti]
                    p.op("act", lambda e, oa=oa, sqw=sqw, ti=ti:
                         e.activation(sqw[:, 0:D], oa[:], AF.Square),
                         reads=[oa], writes=[sqw])
                    p.op("dve", lambda e, sqw=sqw, ti=ti:
                         e.reduce_sum(rms_sc[:, ti, :, 0:1],
                                      sqw[:, 0:D].rearrange("p (h d) -> p h d", h=H), axis=AX.X),
                         reads=[sqw], writes=[rms_sc])
                    p.op("dve", lambda e, ti=ti:
                         e.tensor_scalar(rms_sc[:, ti, :, 1:2], rms_sc[:, ti, :, 0:1], 1.0 / DH, 1e-5,
                                         ALU.mult, ALU.add),
                         reads=[rms_sc], writes=[rms_sc])
                    p.op("act", lambda e, ti=ti:
                         e.activation(rms_sc[:, ti, :, 2:3], rms_sc[:, ti, :, 1:2], AF.Ln),
                         reads=[rms_sc], writes=[rms_sc])
                    p.op("act", lambda e, ti=ti:
                         e.activation(rms_sc[:, ti, :, 3:4], rms_sc[:, ti, :, 2:3], AF.Exp, scale=-0.5),
                         reads=[rms_sc], writes=[rms_sc])
                    p.op("dve", lambda e, oa=oa, ti=ti:
                         e.tensor_mul(o_nb[:, ti, :].rearrange("p (h d) -> p h d", h=H),
                                      oa[:].rearrange("p (h d) -> p h d", h=H),
                                      rms_sc[:, ti, :, 3:4].broadcast_to([128, H, DH])),
                         reads=[oa, rms_sc], writes=[o_nb])
            def p3b_segB2p(s, t0, ls_tok, ll_tok, dl_tok, vv_tok, x_tok3, xb_tok3, xb_f3,
                           sums3, stats3, statsT, sqscr3, tmp3b, sumsAb, rms_sc, hdn, lg3, coef, sc3,
                           o_tok, o_nb, o_f, xa, x2p):
                for ti in range(2):
                    for j in range(8):
                        p.op("pe", lambda e, o_nb=o_nb, ti=ti, j=j:
                             e.transpose(pTb2[:, j * 128:(j + 1) * 128], o_nb[:, ti, j * 128:(j + 1) * 128], identb[:]),
                             reads=[o_nb, identb], writes=[pTb2], inc=(j == 7))
                    p.op("act", lambda e, o_f=o_f, ti=ti:
                         e.copy(o_f[:, :, ti * 128:(ti + 1) * 128], pTb2[:, 0:1024].rearrange("p (j c) -> p j c", j=8)),
                         reads=[pTb2], writes=[o_f])
                for ti in range(2):
                    for hf in range(2):
                        for k in range(8):
                            p.op("pe", lambda e, ti=ti, hf=hf, k=k:
                                 e.matmul(pO3[:, 0:512], o_f[:, k, ti * 128:(ti + 1) * 128],
                                          wo_l[:, k, hf * 512:(hf + 1) * 512], start=(k == 0), stop=(k == 7)),
                                 reads=[o_f, wo_l], writes=[pO3], inc=(k == 7))
                        p.op("dve", lambda e, ti=ti, hf=hf:
                             e.tensor_add(xa[:, ti, hf * 512:(hf + 1) * 512], pO3[:, 0:512],
                                          x_tok3[:, ti, hf * 512:(hf + 1) * 512]),
                             reads=[pO3, x_tok3], writes=[xa])
                p.op("dve", lambda e:
                     e.reduce_sum(ln_sc[:, :, 0:1], xa[:].rearrange("p ti d -> p ti d"), axis=AX.X),
                     reads=[xa], writes=[ln_sc])
                p.op("dve", lambda e:
                     e.tensor_scalar(ln_sc[:, :, 1:2], ln_sc[:, :, 0:1], 1.0 / D, None, ALU.mult, ALU.bypass),
                     reads=[ln_sc], writes=[ln_sc])
                for ti in range(2):
                    sqw = sqBB[ti]
                    p.op("act", lambda e, sqw=sqw, ti=ti:
                         e.activation(sqw[:, 0:D], xa[:, ti, :], AF.Square),
                         reads=[xa], writes=[sqw])
                    p.op("dve", lambda e, sqw=sqw, ti=ti:
                         e.reduce_sum(ln_sc[:, ti:ti + 1, 2:3], sqw[:, 0:D].rearrange("p (o d) -> p o d", o=1),
                                      axis=AX.X),
                         reads=[sqw], writes=[ln_sc])
                p.op("dve", lambda e:
                     e.tensor_mul(ln_sc[:, :, 3:4], ln_sc[:, :, 1:2], ln_sc[:, :, 1:2]),
                     reads=[ln_sc], writes=[ln_sc])
                p.op("dve", lambda e:
                     e.scalar_tensor_tensor(ln_sc[:, :, 4:5], ln_sc[:, :, 2:3], 1.0 / D,
                                            ln_sc[:, :, 3:4], ALU.mult, ALU.subtract),
                     reads=[ln_sc], writes=[ln_sc])
                p.op("dve", lambda e:
                     e.tensor_scalar(ln_sc[:, :, 5:6], ln_sc[:, :, 4:5], 1.0, 1e-5, ALU.mult, ALU.add),
                     reads=[ln_sc], writes=[ln_sc])
                p.op("act", lambda e:
                     e.activation(ln_sc[:, :, 6:7], ln_sc[:, :, 5:6], AF.Ln),
                     reads=[ln_sc], writes=[ln_sc])
                p.op("act", lambda e:
                     e.activation(ln_sc[:, :, 7:8], ln_sc[:, :, 6:7], AF.Exp, scale=-0.5),
                     reads=[ln_sc], writes=[ln_sc])
                for ti in range(2):
                    p.op("dve", lambda e, ti=ti:
                         e.tensor_scalar(x2p[:, ti, :], xa[:, ti, :], ln_sc[:, ti, 1:2], ln_sc[:, ti, 7:8],
                                         ALU.subtract, ALU.mult),
                         reads=[xa, ln_sc], writes=[x2p])
                if "s_dbg1" in debug_taps:
                    p.dma("dact", s_dbg1[t0:t0 + S3, :].rearrange("(ti p) c -> p ti c", p=128),
                          stats3[:].rearrange("p ti h k -> p ti (h k)"))
                if "s_dbg2" in debug_taps:
                    p.dma("dact", s_dbg2[t0:t0 + S3, :].rearrange("(ti p) c -> p ti c", p=128),
                          coef[:].rearrange("p ti h k -> p ti (h k)"))
                if "s_dbg4" in debug_taps:
                    p.dma("dpool", s_dbg4[t0:t0 + S3, :].rearrange("(ti p) d -> p ti d", p=128), o_nb[:])
                if "s_dbg5" in debug_taps and s == NSEG3 - 1:
                    p.dma("dpool", s_dbg5.rearrange("(k p) m -> p k m", p=128), wo_l[:])
                    for kk2 in range(8):
                        p.dma("dpool", s_dbg6[0:S3, kk2 * 128:(kk2 + 1) * 128].rearrange("(ti p) c -> p ti c", p=128),
                              o_f[:, kk2, :].rearrange("p (ti tt) -> p ti tt", ti=2))
                if "s_dbg3" in debug_taps:
                    p.dma("dact", s_dbg3[t0:t0 + S3, :].rearrange("(ti p) d -> p ti d", p=128), xa[:])
                p.dma("dact", s_x2[t0:t0 + S3, :].rearrange("(ti p) d -> p ti d", p=128), x2p[:])

            def p3b_args(s):
                pcs = s % 2
                return (s, s * S3, ls_tokB[pcs], ll_tokB[pcs], dl_tokB[pcs], vv_tokB[pcs],
                        x_tok3B[s % 3], xb_tok3B[pcs], xb_f3B[pcs], sums3B[pcs], stats3B[pcs],
                        statsTB[pcs], sqscr3B[pcs], tmp3bB[pcs], sumsAbB[pcs], rms_scB[pcs], hdnB[pcs], lg3B[pcs],
                        coefB[pcs], sc3B[pcs], o_tokB[pcs], o_nbB[pcs], o_fB[pcs], xaB[pcs], x2pB[pcs])

            if p3b_segs > 0:
                p3b_segA(*p3b_args(0))
                if p3b_segs > 1:
                    p3b_segA(*p3b_args(1))
                p3b_segA2(*p3b_args(0))
                p3b_segB1(*p3b_args(0))
            for s in range(p3b_segs):
                if s + 1 < p3b_segs:
                    p3b_segA2(*p3b_args(s + 1))
                p3b_segB2(*p3b_args(s))
                if s + 1 < p3b_segs:
                    p3b_segB1(*p3b_args(s + 1))
                if s + 2 < p3b_segs:
                    p3b_segA(*p3b_args(s + 2))
                p3b_segB2p(*p3b_args(s))

        p.barrier()

        if max_phase < 4.5:
            p.barrier_dma_done()
            p.emit()
            return nc
        with contextlib.ExitStack() as s4:
            sb4 = lambda name, shape, dt: s4.enter_context(nc.sbuf_tensor(name, shape, dt))
            ps4_ = lambda name, shape, dt: s4.enter_context(nc.psum_tensor(name, shape, dt))

            fw1_l = sb4("fw1_l", [128, 8, FF], BF16)
            fw2_l = sb4("fw2_l", [128, 32, D], BF16)
            fb1s = sb4("fb1s", [128, 32], F32)
            gammab = sb4("gammab", [128, D], F32)
            betab = sb4("betab", [128, D], F32)
            rowbuf = sb4("rowbuf", [1, D], F32)

            x2_tok = [sb4(f"x2_tok{i}", [128, 2, D], F32) for i in range(3)]
            x2_f = [sb4(f"x2_f{i}", [128, 8, S4], BF16) for i in range(2)]
            h_fq = [sb4(f"h_fq{i}", [128, 8, S4], BF16) for i in range(4)]
            y1_s = sb4("y1_s", [128, 2, D], F32)
            y1 = [y1_s, y1_s]
            out_t = [y1_s, y1_s]
            sq4 = sb4("sq4", [128, D], BF16)
            sc4 = [sb4(f"sc4_{i}", [128, 2, 8], F32) for i in range(2)]
            fb2row = sb4("fb2row", [1, D], F32)

            pT4 = ps4_("pT4", [128, 512], F32)
            pT4b = ps4_("pT4b", [128, 512], F32)
            pH4 = ps4_("pH4", [128, 512], F32)
            pH4b = ps4_("pH4b", [128, 512], F32)
            pO4 = ps4_("pO4", [128, 512], F32)
            pO4b = ps4_("pO4b", [128, 512], F32)
            pB4 = ps4_("pB4", [128, 512], F32)

            p.dma("dpool", fw1_l[:], fw1_h.rearrange("(k p) m -> p k m", p=128))
            p.dma("dpool", fw2_l[:], fw2_h.rearrange("(k p) m -> p k m", p=128))
            p.dma("dsp", fb1s[:], fb1f_h)
            p.dma("dsp", fb2row[:], fb2f_h)
            for (row_h, dstb) in ((gamma_h, gammab), (beta_h, betab)):
                p.dma("dsp", rowbuf[:], row_h)
                for hf in range(2):
                    p.op("pe", lambda e, hf=hf: e.matmul(pB4[:, 0:512], ones1f[:], rowbuf[0:1, hf * 512:(hf + 1) * 512],
                                                         start=True, stop=True),
                         reads=[ones1f, rowbuf], writes=[pB4])
                    p.op("act", lambda e, dstb=dstb, hf=hf: e.copy(dstb[:, hf * 512:(hf + 1) * 512], pB4[:, 0:512]),
                         reads=[pB4], writes=[dstb])

            def ln_stats(src_ap, scb, msl, vsl, rsl, reads):
                p.op("dve", lambda e: e.reduce_sum(msl, src_ap, axis=AX.X), reads=reads, writes=[scb])
                p.op("dve", lambda e: e.tensor_scalar(msl, msl, 1.0 / D, None, ALU.mult, ALU.bypass),
                     reads=[scb], writes=[scb])
                p.op("dve", lambda e: e.memset(vsl, 0.0), writes=[scb])
                p.op("act", lambda e: e.activation(sq4[:], src_ap, AF.Square, accum_out=vsl),
                     reads=reads, writes=[sq4, scb])
                p.op("dve", lambda e: e.tensor_mul(rsl, msl, msl), reads=[scb], writes=[scb])
                p.op("dve", lambda e: e.scalar_tensor_tensor(vsl, vsl, 1.0 / D, rsl, ALU.mult, ALU.subtract),
                     reads=[scb], writes=[scb])
                p.op("dve", lambda e: e.tensor_scalar(vsl, vsl, 1.0, 1e-5, ALU.mult, ALU.add),
                     reads=[scb], writes=[scb])
                p.op("act", lambda e: e.activation(rsl, vsl, AF.Ln), reads=[scb], writes=[scb])
                p.op("act", lambda e: e.activation(rsl, rsl, AF.Exp, scale=-0.5), reads=[scb], writes=[scb])

            for s in range(NSEG4):
                t0 = s * S4
                pc = s % 2
                x2t, x2f, yy, scb = x2_tok[s % 3], x2_f[pc], y1[pc], sc4[pc]
                p.dma("dsp", x2t[:], s_x2[t0:t0 + S4, :].rearrange("(ti p) d -> p ti d", p=128))
                for ti in range(2):
                    for jh in range(2):
                        pz4 = pT4 if (ti * 2 + jh) % 2 == 0 else pT4b
                        for j in range(4):
                            p.op("pe", lambda e, pz4=pz4, x2t=x2t, ti=ti, jh=jh, j=j:
                                 e.transpose(pz4[:, j * 128:(j + 1) * 128],
                                             x2t[:, ti, (jh * 4 + j) * 128:(jh * 4 + j + 1) * 128], identf[:]),
                                 reads=[x2t, identf], writes=[pz4], inc=(j == 3))
                        p.op("act", lambda e, pz4=pz4, x2f=x2f, ti=ti, jh=jh:
                             e.copy(x2f[:, jh * 4:(jh + 1) * 4, ti * 128:(ti + 1) * 128],
                                    pz4[:, 0:512].rearrange("p (j c) -> p j c", j=4)),
                             reads=[pz4], writes=[x2f])
                for mt in range(32):
                    pp = pH4 if mt % 2 == 0 else pH4b
                    hq = h_fq[mt // 8]
                    for k in range(8):
                        p.op("pe", lambda e, pp=pp, x2f=x2f, mt=mt, k=k:
                             e.matmul(pp[:, 0:S4], fw1_l[:, k, mt * 128:(mt + 1) * 128], x2f[:, k, :],
                                      start=(k == 0), stop=(k == 7)),
                             reads=[fw1_l, x2f], writes=[pp], inc=(k == 7))
                    p.op("act", lambda e, pp=pp, hq=hq, mt=mt:
                         e.activation(hq[:, mt % 8, :], pp[:, 0:S4], AF.Relu, bias=fb1s[:, mt:mt + 1]),
                         reads=[pp, fb1s], writes=[hq])
                for ti in range(2):
                    for hf in range(2):
                        pq = pO4 if (ti * 2 + hf) % 2 == 0 else pO4b
                        for k in range(32):
                            hq = h_fq[k // 8]
                            p.op("pe", lambda e, pq=pq, hq=hq, ti=ti, hf=hf, k=k:
                                 e.matmul(pq[:, 0:512], hq[:, k % 8, ti * 128:(ti + 1) * 128],
                                          fw2_l[:, k, hf * 512:(hf + 1) * 512], start=(k == 0), stop=False),
                                 reads=[hq, fw2_l], writes=[pq], inc=False)
                        p.op("pe", lambda e, pq=pq, hf=hf:
                             e.matmul(pq[:, 0:512], ones1f[:], fb2row[0:1, hf * 512:(hf + 1) * 512],
                                      start=False, stop=True),
                             reads=[ones1f, fb2row], writes=[pq], inc=True)
                        p.op("dve", lambda e, pq=pq, yy=yy, x2t=x2t, ti=ti, hf=hf:
                             e.tensor_add(yy[:, ti, hf * 512:(hf + 1) * 512], pq[:, 0:512],
                                          x2t[:, ti, hf * 512:(hf + 1) * 512]),
                             reads=[pq, x2t], writes=[yy])
                    ln_stats(yy[:, ti, :], scb, scb[:, ti, 0:1], scb[:, ti, 1:2], scb[:, ti, 2:3], [yy])
                    p.op("dve", lambda e, yy=yy, scb=scb, ti=ti:
                         e.tensor_scalar(yy[:, ti, :], yy[:, ti, :], scb[:, ti, 0:1], scb[:, ti, 2:3],
                                         ALU.subtract, ALU.mult),
                         reads=[yy, scb], writes=[yy])
                    p.op("dve", lambda e, yy=yy, ti=ti:
                         e.tensor_mul(yy[:, ti, :], yy[:, ti, :], gammab[:]),
                         reads=[yy, gammab], writes=[yy])
                    p.op("dve", lambda e, yy=yy, ti=ti:
                         e.tensor_add(yy[:, ti, :], yy[:, ti, :], betab[:]),
                         reads=[yy, betab], writes=[yy])
                    p.op("dve", lambda e, yy=yy, x2t=x2t, ti=ti:
                         e.tensor_add(yy[:, ti, :], yy[:, ti, :], x2t[:, ti, :]),
                         reads=[yy, x2t], writes=[yy])
                    ln_stats(yy[:, ti, :], scb, scb[:, ti, 3:4], scb[:, ti, 4:5], scb[:, ti, 5:6], [yy])
                    p.op("dve", lambda e, yy=yy, scb=scb, ti=ti:
                         e.tensor_scalar(yy[:, ti, :], yy[:, ti, :], scb[:, ti, 3:4], scb[:, ti, 5:6],
                                         ALU.subtract, ALU.mult),
                         reads=[yy, scb], writes=[yy])
                p.dma("dact", out_h[t0:t0 + S4, :].rearrange("(ti p) d -> p ti d", p=128), yy[:])

        p.barrier()
        p.barrier_dma_done()
        p.emit()
    return nc



def _layout_inputs(inputs):
    g = lambda k: np.ascontiguousarray(inputs[k])
    fw1 = inputs["L0_ffn_w1"][0]
    fw2 = inputs["L0_ffn_w2"][0]
    fb1 = inputs["L0_ffn_b1"][0]
    fb2 = inputs["L0_ffn_b2"][0]
    fuse_w1 = inputs["L0_attn_fuse_w1"]
    common = {
        "wq_t": np.ascontiguousarray(inputs["L0_attn_q_proj_w"].T),
        "wk_t": np.ascontiguousarray(inputs["L0_attn_k_proj_w"].T),
        "wv_t": np.ascontiguousarray(inputs["L0_attn_v_proj_w"].T),
        "wb_t": np.ascontiguousarray(inputs["L0_attn_b_proj_w"].T),
        "qconv": g("L0_attn_q_conv_w"),
        "kconv": g("L0_attn_k_conv_w"),
        "vconv": g("L0_attn_v_conv_w"),
        "firS": np.ascontiguousarray(inputs["L0_attn_fir_short_w"].reshape(D, 5)),
        "firL": np.ascontiguousarray(inputs["L0_attn_fir_long_w"].reshape(D, 64)),
        "w1x_t": np.ascontiguousarray(fuse_w1[:, :D].T),
        "w1s_t": np.ascontiguousarray(fuse_w1[:, D:D + 16].T[[(r % 4) * 4 + r // 4 for r in range(16)]]),
        "fuse_b1v": np.ascontiguousarray(inputs["L0_attn_fuse_b1"].reshape(8, 128).T),
        "w2g_t": np.ascontiguousarray(inputs["L0_attn_fuse_w2"].T),
        "fuse_b2v": np.ascontiguousarray(inputs["L0_attn_fuse_b2"].reshape(1, H)),
        "ltemp": np.ascontiguousarray(inputs["L0_attn_logit_temp"].reshape(1, 1)),
        "rs": np.ascontiguousarray(inputs["L0_attn_res_scale_short"].reshape(1, 1)),
        "rl": np.ascontiguousarray(inputs["L0_attn_res_scale_long"].reshape(1, 1)),
        "onorm": np.ascontiguousarray(inputs["L0_attn_o_norm_w"].reshape(2, 128).T),
        "wo_t": np.ascontiguousarray(inputs["L0_attn_o_proj_w"].T),
        "fw1": np.ascontiguousarray(fw1),
        "fb1": np.ascontiguousarray(fb1.reshape(32, 128).T),
        "fw2": np.ascontiguousarray(fw2),
        "fb2": np.ascontiguousarray(fb2.reshape(1, D)),
        "gamma": np.ascontiguousarray(inputs["L0_ffn_gamma"].reshape(1, D)),
        "beta": np.ascontiguousarray(inputs["L0_ffn_beta"].reshape(1, D)),
    }
    common = {k: v.astype(np.float32, copy=False) for k, v in common.items()}
    x = inputs["L0_attn_hidden_states"]
    in_maps = []
    for b in range(8):
        m = dict(common)
        m["x"] = np.ascontiguousarray(x[b]).astype(np.float32, copy=False)
        in_maps.append(m)
    return in_maps


def kernel(**inputs):
    in_maps = _layout_inputs(inputs)
    nc = build_program()
    res = run_bass_kernel_spmd(nc, in_maps, core_ids=list(range(8)))
    out = np.stack([r["out"] for r in res.results], axis=0)
    return out.astype(np.float32)
